# version 10
# baseline (speedup 1.0000x reference)
import contextlib
import numpy as np
import ml_dtypes
import concourse.bass as bass
import concourse.mybir as mybir
from concourse.bass_utils import run_bass_kernel_spmd

F32 = mybir.dt.float32
BF16 = mybir.dt.bfloat16
AF = mybir.ActivationFunctionType
ALU = mybir.AluOpType

D = 1024
FCH = 8
T = 2048
G = 256
NG = 8
E = 2048
ECH = 16
QKD = 128
FF = 16384
OCH = 8
EPS = 1e-5
NEGBIG = -1.0e9
BIAS_SCALE = float(np.sqrt(128.0))
NCORES = 8

NHF = 2
HT = 1024
SLF = 1024
NSL = FF // SLF
KCH = SLF // 128
STR = SLF // 512
ZPS = STR * 4 * 2
CPS = 2 * 2 * 4
NKS = NHF * NSL

import os
STAGE = int(os.environ.get("MK_STAGE", "3"))
DEBUG = os.environ.get("MK_DEBUG", "0") == "1"
TRUNC = int(os.environ.get("MK_TRUNC", "0"))


def _bucket_of_delta(delta: int) -> int:
    n = max(delta, 0)
    if n < 16:
        return n
    val = 16 + int(np.log(max(n, 1) / 16.0) / np.log(128.0 / 16.0) * 16.0)
    return min(val, 31)


def host_constants():
    M = np.zeros((32, 128, 512), dtype=np.float32)
    NEG = np.zeros((128, 512), dtype=np.float32)
    for jc in range(2):
        for j in range(128):
            for i in range(256):
                d_ = i - j - 128 * jc
                if d_ < 0:
                    NEG[j, jc * 256 + i] = NEGBIG
                else:
                    M[_bucket_of_delta(d_), j, jc * 256 + i] = 1.0
    return M, NEG, np.eye(128, dtype=np.float32)


def prep_inputs(inputs):
    a = {k: np.asarray(v) for k, v in inputs.items()}
    x = a["L0_attn_x"][0]
    Wh = a["L0_attn_Wh"]; Wqk = a["L0_attn_Wqk"]; Wout = a["L0_attn_Wout"]
    bh = a["L0_attn_bh"]; bqk = a["L0_attn_bqk"]; bout = a["L0_attn_bout"]
    qk_w = a["L0_attn_qk_w"]; qk_b = a["L0_attn_qk_b"]
    lnw = a["L0_attn_ln_w"]; lnb = a["L0_attn_ln_b"]
    rel = a["L0_attn_rel_table"]
    maker = a["L0_ffn_rail_maker"]; up = a["L0_ffn_rail_up_proj"]; wc = a["L0_ffn_w_cproj"]

    M, NEG, ident = host_constants()
    makerbd = np.zeros((32, 128, 128), dtype=np.float32)
    upbd = np.zeros((32, 128, 512), dtype=np.float32)
    for h in range(4):
        for cc in range(8):
            idx = h * 8 + cc
            for b in range(8):
                r = 8 * cc + b
                makerbd[idx, 16 * b:16 * b + 16, 16 * b:16 * b + 16] = maker[h, r].T
                upbd[idx, 16 * b:16 * b + 16, 64 * b:64 * b + 64] = up[h, r].T
    common = {
        "whva": np.ascontiguousarray(Wh[:, 0:1024]),
        "whvb": np.ascontiguousarray(Wh[:, 1024:2048]),
        "whga": np.ascontiguousarray(Wh[:, E:E + 1024]),
        "whgb": np.ascontiguousarray(Wh[:, E + 1024:]),
        "bhv": np.ascontiguousarray(bh[:E].reshape(1, E)),
        "bhg": np.ascontiguousarray(bh[E:].reshape(ECH, 128).T),
        "wqk": np.ascontiguousarray(Wqk),
        "bqk": np.ascontiguousarray(bqk.reshape(128, 1)),
        "qkw": np.ascontiguousarray(qk_w.T),
        "qkb": np.ascontiguousarray(qk_b.T),
        "lnw": np.ascontiguousarray(lnw.reshape(FCH, 128).T),
        "lnb": np.ascontiguousarray(lnb.reshape(FCH, 128).T),
        "rel": np.ascontiguousarray(rel.reshape(1, 32)),
        "wout": np.ascontiguousarray(Wout),
        "bout": np.ascontiguousarray(bout.reshape(1, D)),
        "masks": M, "neg": NEG, "ident": ident,
        "makerbd": makerbd, "upbd": upbd,
        "wct": np.ascontiguousarray(wc.T),
    }
    in_maps = []
    for c in range(NCORES):
        m = dict(common)
        m["x0t"] = np.ascontiguousarray(x[c * T:(c + 1) * T, :].T)
        pm = np.zeros((128, 8), dtype=np.float32)
        pm[:, :c] = 1.0
        m["pmask"] = pm
        in_maps.append(m)
    return in_maps


def chunkT(ap, nch, row_elems, ch0=0, nsub=None):
    if nsub is None:
        nsub = nch
    return bass.AP(ap.tensor, ch0 * 128 * row_elems,
                   [[row_elems, 128], [128 * row_elems, nsub], [1, row_elems]])


def build_nc(stage=STAGE, debug=DEBUG):
    nc = bass.Bass("TRN2", target_bir_lowering=False, debug=False,
                   detect_race_conditions=False)

    x0t = nc.dram_tensor("x0t", [D, T], F32, kind="ExternalInput").ap()
    whva_d = nc.dram_tensor("whva", [D, 1024], F32, kind="ExternalInput").ap()
    whvb_d = nc.dram_tensor("whvb", [D, 1024], F32, kind="ExternalInput").ap()
    whga_d = nc.dram_tensor("whga", [D, 1024], F32, kind="ExternalInput").ap()
    whgb_d = nc.dram_tensor("whgb", [D, 1024], F32, kind="ExternalInput").ap()
    bhv_d = nc.dram_tensor("bhv", [1, E], F32, kind="ExternalInput").ap()
    bhg_d = nc.dram_tensor("bhg", [128, ECH], F32, kind="ExternalInput").ap()
    wqk_d = nc.dram_tensor("wqk", [D, QKD], F32, kind="ExternalInput").ap()
    bqk_d = nc.dram_tensor("bqk", [128, 1], F32, kind="ExternalInput").ap()
    qkw_d = nc.dram_tensor("qkw", [128, 4], F32, kind="ExternalInput").ap()
    qkb_d = nc.dram_tensor("qkb", [128, 4], F32, kind="ExternalInput").ap()
    lnw_d = nc.dram_tensor("lnw", [128, FCH], F32, kind="ExternalInput").ap()
    lnb_d = nc.dram_tensor("lnb", [128, FCH], F32, kind="ExternalInput").ap()
    rel_d = nc.dram_tensor("rel", [1, 32], F32, kind="ExternalInput").ap()
    wout_d = nc.dram_tensor("wout", [E, D], F32, kind="ExternalInput").ap()
    bout_d = nc.dram_tensor("bout", [1, D], F32, kind="ExternalInput").ap()
    masks_d = nc.dram_tensor("masks", [32, 128, 512], F32, kind="ExternalInput").ap()
    neg_d = nc.dram_tensor("neg", [128, 512], F32, kind="ExternalInput").ap()
    ident_d = nc.dram_tensor("ident", [128, 128], F32, kind="ExternalInput").ap()
    makerbd_d = nc.dram_tensor("makerbd", [32, 128, 128], F32, kind="ExternalInput").ap()
    upbd_d = nc.dram_tensor("upbd", [32, 128, 512], F32, kind="ExternalInput").ap()
    wct_d = nc.dram_tensor("wct", [FF, D], F32, kind="ExternalInput").ap()
    pmask_d = nc.dram_tensor("pmask", [128, 8], F32, kind="ExternalInput").ap()

    out_d = nc.dram_tensor("out", [D, T], F32, kind="ExternalOutput").ap()
    spill = "ExternalOutput" if debug else "Internal"
    v_sp = nc.dram_tensor("v_sp", [NG, 128, 2 * E], BF16, kind=spill).ap()
    p_sp = nc.dram_tensor("p_sp", [NG, 128, E], BF16, kind=spill).ap()
    nsc_sp = nc.dram_tensor("nsc_sp", [128, FCH * T], BF16, kind=spill).ap()
    x1_sp = nc.dram_tensor("x1_sp", [NG, 128, OCH * G], F32, kind=spill).ap()
    cc_in = nc.dram_tensor("cc_in", [128, E], BF16)
    cc_out = nc.dram_tensor("cc_out", [NCORES * 128, E], BF16, addr_space="Shared")
    cc_bar_in = nc.dram_tensor("cc_bar_in", [1, 16], BF16)
    cc_bar_out = nc.dram_tensor("cc_bar_out", [NCORES, 16], BF16, addr_space="Shared")
    if debug:
        dbg1 = nc.dram_tensor("dbg1", [128, T], F32, kind="ExternalOutput").ap()
        dbg2 = nc.dram_tensor("dbg2", [128, 512], F32, kind="ExternalOutput").ap()
        dbg3 = nc.dram_tensor("dbg3", [8, T], F32, kind="ExternalOutput").ap()
        dbg4 = nc.dram_tensor("dbg4", [128, 4 * T], BF16, kind="ExternalOutput").ap()
        dbg5 = nc.dram_tensor("dbg5", [128, E], F32, kind="ExternalOutput").ap()

    es = contextlib.ExitStack()
    es012 = contextlib.ExitStack()
    es0 = contextlib.ExitStack()
    es01 = contextlib.ExitStack()

    def mk(stk, name, shape, dt=F32):
        return stk.enter_context(nc.sbuf_tensor(name, shape, dt))
    def sem(name):
        return es.enter_context(nc.semaphore(name))

    bhg_sb = mk(es, "bhg_sb", [128, ECH])
    bqk_sb = mk(es, "bqk_sb", [128, 1])
    qkw_sb = mk(es, "qkw_sb", [128, 4])
    qkb_sb = mk(es, "qkb_sb", [128, 4])
    qkws_sb = mk(es, "qkws_sb", [128, 2])
    lnw_sb = mk(es, "lnw_sb", [128, FCH])
    lnb_sb = mk(es, "lnb_sb", [128, FCH])
    rel_sb = mk(es, "rel_sb", [1, 32])
    rels_sb = mk(es, "rels_sb", [1, 32])
    identb_sb = mk(es, "identb_sb", [128, 128], BF16)
    pmask_sb = mk(es, "pmask_sb", [128, 8])
    ones1f = mk(es, "ones1f", [1, 128])
    ones1b = mk(es, "ones1b", [1, 128], BF16)
    onesmatb = mk(es, "onesmatb", [128, 128], BF16)
    onesrowb = mk(es, "onesrowb", [1, G], BF16)
    bhv_sb = mk(es, "bhv_sb", [1, E], BF16)
    bout_sb = mk(es, "bout_sb", [1, D], BF16)
    q_sum = mk(es, "q_sum", [128, T])
    q_sq = mk(es, "q_sq", [128, T])

    qq_sb = mk(es012, "qq_sb", [128, T], BF16)
    qkk_sb = mk(es012, "qkk_sb", [128, T], BF16)
    lq_sb = mk(es012, "lq_sb", [128, T], BF16)
    biasT_sb = mk(es012, "biasT_sb", [128, 512])
    tcol_sb = mk(es012, "tcol_sb", [128, 32])
    bcA1_sb = mk(es012, "bcA1_sb", [128, T])
    S_sb = mk(es012, "S_sb", [128, E], BF16)
    negb_sb = mk(es012, "negb_sb", [1, T], BF16)
    whg0_sb = mk(es012, "whg0_sb", [128, FCH * 1024], BF16)

    nsc_sb = mk(es01, "nsc_sb", [128, FCH * T], BF16)
    lkt_sb = mk(es01, "lkt_sb", [128, 16 * 128], BF16)
    wqks_sb = mk(es01, "wqks_sb", [128, FCH * QKD], BF16)
    whv_sb = mk(es01, "whv_sb", [128, FCH * E], BF16)

    x0b_sb = [mk(es0, f"x0b_sb{i}", [128, T], BF16) for i in range(2)]
    xsqb_sb = mk(es0, "xsqb_sb", [128, T], BF16)
    x0f_sb = [mk(es0, f"x0f_sb{i}", [128, T]) for i in range(2)]
    xsq_sb = mk(es0, "xsq_sb", [128, T])
    lkf_sb = mk(es0, "lkf_sb", [128, T], BF16)
    r0 = mk(es0, "r0", [128, T]); r1 = mk(es0, "r1", [128, T])
    r2 = mk(es0, "r2", [128, T])

    d_par = sem("d_par"); d_parg = sem("d_parg"); d_x0 = sem("d_x0")
    d_msk = [sem("d_msk0"), sem("d_msk1")]
    d_wqk = sem("d_wqk")
    d_whv = [sem("d_whva"), sem("d_whvb")]
    d_whg = [sem("d_whga"), sem("d_whgb")]
    d_wout = sem("d_wout")
    d_x0b = [sem("d_x0b0"), sem("d_x0b1")]
    d_x0f = [sem("d_x0f0"), sem("d_x0f1")]
    d_x1b = [sem("d_x1b0"), sem("d_x1b1")]
    d_vw = [sem("d_vw0"), sem("d_vw1")]
    d_pw = [sem("d_pw0"), sem("d_pw1")]
    d_nw = sem("d_nw")
    d_vr = [sem("d_vr0"), sem("d_vr1")]
    d_pr = [sem("d_pr0"), sem("d_pr1")]
    d_nr = [sem("d_nr0"), sem("d_nr1")]
    d_x0g = [sem("d_x0g0"), sem("d_x0g1")]
    d_x1w = [sem("d_x1w0"), sem("d_x1w1")]
    d_ci = sem("d_ci")
    d_co = [sem("d_co0"), sem("d_co1")]
    d_mk = sem("d_mk")
    d_up = [sem("d_up0"), sem("d_up1")]
    d_w3 = [sem("d_w30"), sem("d_w31")]
    d_x1r = [sem("d_x1r0"), sem("d_x1r1")]
    d_x1h = [sem("d_x1h0"), sem("d_x1h1")]
    d_out = sem("d_out"); d_dbg = sem("d_dbg")
    v_ones = sem("v_ones"); s_sq = sem("s_sq"); t_st = sem("t_st")
    v_r1 = sem("v_r1"); s_r1 = sem("s_r1"); v_r2 = sem("v_r2"); s_r2 = sem("s_r2")
    v_rows = sem("v_rows"); t_bc = sem("t_bc"); v_bc = sem("v_bc")
    v_nsc = sem("v_nsc"); t_qk = sem("t_qk"); s_qk = sem("s_qk")
    v_quads = sem("v_quads"); t_lkT = sem("t_lkT"); v_lkT = sem("v_lkT")
    s_rel = sem("s_rel"); t_tcol = sem("t_tcol"); v_tcol = sem("v_tcol")
    v_msk = sem("v_msk"); v_bias = sem("v_bias")
    t_v = sem("t_v"); s_vsil = sem("s_vsil"); t_lin = sem("t_lin"); v_lkv = sem("v_lkv")
    s_snap = sem("s_snap"); s_L = sem("s_L"); cc_done = sem("cc_done")
    v_Ss = sem("v_Ss")
    t_sim = sem("t_sim"); v_sim1 = sem("v_sim1"); s_relu = sem("s_relu")
    v_atn = sem("v_atn"); t_gate = sem("t_gate"); s_gate = sem("s_gate")
    t_B = sem("t_B"); v_hid = sem("v_hid"); t_Bg = sem("t_Bg")
    t_wout = sem("t_wout"); v_x1 = sem("v_x1")
    s_sq3 = sem("s_sq3"); t_st3 = sem("t_st3")
    v_r31 = sem("v_r31"); s_r31 = sem("s_r31"); v_rows3 = sem("v_rows3")
    t_bc3 = sem("t_bc3"); v_bc3 = sem("v_bc3"); v_n3 = sem("v_n3")
    t_D = sem("t_D"); v_D = sem("v_D")
    t_z = sem("t_z"); s_z = sem("s_z"); t_cp = sem("t_cp"); v_acc = sem("v_acc")
    v_out = sem("v_out")

    NPAR = 8

    ps_st = contextlib.ExitStack()
    st1_ps = ps_st.enter_context(nc.psum_tensor("st1_ps", [128, T], F32))
    st2_ps = ps_st.enter_context(nc.psum_tensor("st2_ps", [128, T], F32))

    with nc.Block() as blk:
        @blk.sync
        def _(sy):
            for dst, src in [(bhg_sb, bhg_d), (bqk_sb, bqk_d), (qkw_sb, qkw_d),
                             (qkb_sb, qkb_d), (lnw_sb, lnw_d), (lnb_sb, lnb_d),
                             (rel_sb, rel_d), (pmask_sb, pmask_d)]:
                sy.dma_start(out=dst[:], in_=src[:, :]).then_inc(d_par, 16)

        @blk.gpsimd
        def _(g_):
            for ch in range(FCH):
                if ch >= 2:
                    g_.wait_ge(t_st, ch - 1)
                g_.dma_start(out=x0b_sb[ch % 2][:, :],
                             in_=x0t[ch * 128:(ch + 1) * 128, :]
                             ).then_inc(d_x0b[ch % 2], 16)
            g_.dma_start(out=wqks_sb[:, :],
                         in_=chunkT(wqk_d, FCH, QKD)).then_inc(d_wqk, 16)
            g_.dma_start(out=bhv_sb[:, :], in_=bhv_d[:, :]).then_inc(d_parg, 16)
            g_.dma_start(out=bout_sb[:, :], in_=bout_d[:, :]).then_inc(d_parg, 16)
            g_.dma_start(out=identb_sb[:, :], in_=ident_d[:, :]).then_inc(d_parg, 16)
            g_.dma_start(out=cc_bar_in[0:1, :], in_=x0t[0:1, 0:16]).then_inc(d_parg, 16)
            for half, src in ((0, whva_d), (1, whvb_d)):
                g_.dma_start(
                    out=bass.AP(whv_sb.ap().tensor, half * 1024,
                                [[FCH * E, 128], [E, FCH], [1, 1024]]),
                    in_=chunkT(src, FCH, 1024),
                ).then_inc(d_whv[half], 16)
            g_.wait_ge(d_parg, 64)
            g_.collective_compute(
                "AllGather", ALU.bypass,
                replica_groups=[list(range(NCORES))],
                ins=[cc_bar_in.ap().opt()],
                outs=[cc_bar_out.ap().opt()],
            ).then_inc(cc_done, 1)
            g_.wait_ge(cc_done, 1)

        @blk.vector
        def _(v):
            v.memset(ones1f[:], 1.0)
            v.memset(ones1b[:], 1.0)
            v.memset(onesmatb[:], 1.0)
            v.memset(onesrowb[:], 1.0).then_inc(v_ones, 1)
            v.wait_ge(d_par, 16 * NPAR)
            v.wait_ge(d_parg, 48)
            v.tensor_scalar(qkws_sb[:, 0:1], qkw_sb[:, 0:1], 1.0 / G, None, ALU.mult)
            v.tensor_scalar(qkws_sb[:, 1:2], qkb_sb[:, 0:1], 1.0 / G, None, ALU.mult).then_inc(v_ones, 1)

        @blk.scalar
        def _(s):
            for ch in range(FCH):
                s.wait_ge(d_x0b[ch % 2], 16 * (ch // 2 + 1))
                if ch >= 1:
                    s.wait_ge(t_st, ch)
                s.activation(xsqb_sb[:, :], x0b_sb[ch % 2][:, :],
                             AF.Square).then_inc(s_sq, 1)
            s.wait_ge(d_par, 16 * NPAR)
            s.wait_ge(d_parg, 48)
            s.activation(rels_sb[:, :], rel_sb[:, :], AF.Copy, scale=BIAS_SCALE).then_inc(s_rel, 1)
            s.wait_ge(v_r1, 1)
            s.activation(r2[:, :], xsq_sb[:, :], AF.Sqrt).then_inc(s_r1, 1)
            s.wait_ge(v_r2, 1)
            s.activation(r1[:, :], xsq_sb[:, :], AF.Sqrt).then_inc(s_r2, 1)

        @blk.tensor
        def _(t):
            t.wait_ge(v_ones, 1)
            for ch in range(FCH):
                t.wait_ge(d_x0b[ch % 2], 16 * (ch // 2 + 1))
                for sl in range(4):
                    t.matmul(st1_ps[:, sl * 512:(sl + 1) * 512], onesmatb[:],
                             x0b_sb[ch % 2][:, sl * 512:(sl + 1) * 512],
                             start=(ch == 0), stop=(ch == FCH - 1))
                t.wait_ge(s_sq, ch + 1)
                for sl in range(4):
                    ins = t.matmul(st2_ps[:, sl * 512:(sl + 1) * 512], onesmatb[:],
                                   xsqb_sb[:, sl * 512:(sl + 1) * 512],
                                   start=(ch == 0), stop=(ch == FCH - 1))
                ins.then_inc(t_st, 1)

        @blk.vector
        def _(v):
            v.wait_ge(t_st, FCH)
            v.tensor_scalar(r0[:, :], st1_ps[:, :], 1.0 / D, None, ALU.mult)
            v.tensor_scalar(r1[:, :], st2_ps[:, :], 1.0 / D, None, ALU.mult)
            v.tensor_mul(r2[:, :], r0[:, :], r0[:, :])
            v.tensor_sub(r1[:, :], r1[:, :], r2[:, :])
            v.tensor_scalar(r2[:, :], r1[:, :], EPS, None, ALU.add)
            v.reciprocal(xsq_sb[:, :], r2[:, :]).then_inc(v_r1, 1)
            v.wait_ge(s_r1, 1)
            v.tensor_mul(r0[:, :], r0[:, :], r2[:, :])
            v.tensor_mul(r1[:, :], r1[:, :], xsq_sb[:, :])
            v.tensor_scalar(r1[:, :], r1[:, :], EPS, None, ALU.add)
            v.reciprocal(xsq_sb[:, :], r1[:, :]).then_inc(v_r2, 1)
            v.wait_ge(s_r2, 1)
            v.tensor_scalar(negb_sb[:, :], r0[0:1, :], -1.0, None, ALU.mult)
            v.tensor_mul(r0[:, :], r0[:, :], r1[:, :])
            v.tensor_mul(r1[:, :], r2[:, :], r1[:, :])
            v.tensor_scalar(r2[:, :], r2[:, :], 1.0, None, ALU.add).then_inc(v_rows, 1)

    if TRUNC == 1:
        with nc.Block() as blk:
            @blk.sync
            def _(sy):
                sy.wait_ge(t_st, FCH)
                sy.wait_ge(v_rows, 1)
                sy.wait_ge(s_rel, 1)
        ps_st.close(); es0.close(); es01.close(); es012.close(); es.close()
        return nc

    ps_st.close()
    ps_qk = contextlib.ExitStack()
    qk_ps = ps_qk.enter_context(nc.psum_tensor("qk_ps", [128, T], F32))

    with nc.Block() as blk:
        @blk.sync
        def _(sy):
            for ch in range(FCH):
                if ch >= 2:
                    sy.wait_ge(v_nsc, ch - 1)
                sy.dma_start(out=x0f_sb[ch % 2][:, :],
                             in_=x0t[ch * 128:(ch + 1) * 128, :]
                             ).then_inc(d_x0f[ch % 2], 16)

        @blk.vector
        def _(v):
            for ch in range(FCH):
                v.wait_ge(d_x0f[ch % 2], 16 * (ch // 2 + 1))
                v.tensor_mul(xsq_sb[:, :], x0f_sb[ch % 2][:, :], r1[:, :])
                v.tensor_sub(xsq_sb[:, :], xsq_sb[:, :], r0[:, :])
                v.tensor_scalar(nsc_sb[:, ch * T:(ch + 1) * T], xsq_sb[:, :],
                                lnw_sb[:, ch:ch + 1], lnb_sb[:, ch:ch + 1],
                                ALU.mult, ALU.add).then_inc(v_nsc, 1)
            v.tensor_copy(bcA1_sb[:, :], r2[:, :]).then_inc(v_bc, 1)

        @blk.tensor
        def _(t):
            t.wait_ge(v_r1, 1)
            t.wait_ge(v_rows, 1)
            t.wait_ge(d_wqk, 16)
            for ch in range(FCH):
                t.wait_ge(v_nsc, ch + 1)
                for sl in range(4):
                    t.matmul(qk_ps[:, sl * 512:(sl + 1) * 512],
                             wqks_sb[:, ch * QKD:(ch + 1) * QKD],
                             nsc_sb[:, ch * T + sl * 512: ch * T + (sl + 1) * 512],
                             start=(ch == 0), stop=(ch == FCH - 1))
            t.nop().then_inc(t_qk, 1)

        @blk.scalar
        def _(s):
            s.wait_ge(t_qk, 1)
            s.wait_ge(v_nsc, FCH)
            s.activation(xsq_sb[:, :], qk_ps[:, :], AF.Silu, bias=bqk_sb[:, 0:1]).then_inc(s_qk, 1)

        @blk.vector
        def _(v):
            v.wait_ge(s_qk, 1)
            v.tensor_scalar(qq_sb[:, :], xsq_sb[:, :], qkws_sb[:, 0:1], qkws_sb[:, 1:2],
                            ALU.mult, ALU.add).then_inc(v_quads, 1)
            v.tensor_scalar(lq_sb[:, :], xsq_sb[:, :], qkw_sb[:, 1:2], qkb_sb[:, 1:2],
                            ALU.mult, ALU.add).then_inc(v_quads, 1)
            v.tensor_scalar(qkk_sb[:, :], xsq_sb[:, :], qkw_sb[:, 2:3], qkb_sb[:, 2:3],
                            ALU.mult, ALU.add).then_inc(v_quads, 1)
            v.tensor_scalar(lkf_sb[:, :], xsq_sb[:, :], qkw_sb[:, 3:4], qkb_sb[:, 3:4],
                            ALU.mult, ALU.add).then_inc(v_quads, 1)

    if TRUNC == 2:
        with nc.Block() as blk:
            @blk.sync
            def _(sy):
                sy.wait_ge(v_nsc, FCH)
                sy.wait_ge(t_qk, 1)
                sy.wait_ge(s_qk, 1)
                sy.wait_ge(v_quads, 4)
                sy.wait_ge(v_bc, 1)
        ps_qk.close(); es0.close(); es01.close(); es012.close(); es.close()
        return nc

    ps_lkt = contextlib.ExitStack()
    lkt_ps = [ps_lkt.enter_context(nc.psum_tensor(f"lkt_ps{i}", [128, 128], BF16))
              for i in range(2)]
    tcol_ps = ps_lkt.enter_context(nc.psum_tensor("tcol_ps", [128, 32], F32))

    with nc.Block() as blk:
        @blk.tensor
        def _(t):
            t.wait_ge(s_rel, 1)
            t.matmul(tcol_ps[:, :], ones1f[:], rels_sb[0:1, :],
                     start=True, stop=True).then_inc(t_tcol, 1)
            t.wait_ge(v_quads, 4)
            for tc in range(16):
                if tc >= 2:
                    t.wait_ge(v_lkT, tc - 1)
                t.transpose(lkt_ps[tc % 2][:, :], lkf_sb[:, tc * 128:(tc + 1) * 128],
                            identb_sb[:]).then_inc(t_lkT, 1)

        @blk.vector
        def _(v):
            v.wait_ge(t_tcol, 1)
            v.tensor_copy(tcol_sb[:, :], tcol_ps[:, :]).then_inc(v_tcol, 1)
            for tc in range(16):
                v.wait_ge(t_lkT, tc + 1)
                v.tensor_copy(lkt_sb[:, tc * 128:(tc + 1) * 128], lkt_ps[tc % 2][:, :]).then_inc(v_lkT, 1)

    ps_lkt.close()

    if TRUNC == 3:
        with nc.Block() as blk:
            @blk.sync
            def _(sy):
                sy.wait_ge(v_tcol, 1)
                sy.wait_ge(v_lkT, 16)
        ps_qk.close(); es0.close(); es01.close(); es012.close(); es.close()
        return nc

    if stage == 0:
        with nc.Block() as blk:
            @blk.sync
            def _(sy):
                n = 0
                if debug:
                    sy.wait_ge(v_nsc, FCH)
                    sy.dma_start(out=nsc_sp[:, :], in_=nsc_sb[:, :]).then_inc(d_dbg, 16); n += 1
                    sy.wait_ge(s_qk, 1)
                    sy.dma_start(out=dbg1[:, :], in_=xsq_sb[:, :]).then_inc(d_dbg, 16); n += 1
                    sy.wait_ge(v_rows, 1)
                    n += 1
                    for r_ in (r0, r1, r2):
                        sy.dma_start(out=dbg3[n - 3:n - 2, :], in_=r_[0:1, :]).then_inc(d_dbg, 16); n += 1
                    sy.wait_ge(v_quads, 4)
                    sy.wait_ge(v_lkT, 16)
                    sy.dma_start(out=dbg4[:, 0:T], in_=qq_sb[:, :]).then_inc(d_dbg, 16); n += 1
                    sy.dma_start(out=dbg4[:, T:2 * T], in_=qkk_sb[:, :]).then_inc(d_dbg, 16); n += 1
                    sy.dma_start(out=dbg4[:, 2 * T:3 * T], in_=lq_sb[:, :]).then_inc(d_dbg, 16); n += 1
                    sy.dma_start(out=dbg4[:, 3 * T:4 * T], in_=lkt_sb[:, :]).then_inc(d_dbg, 16); n += 1
                    sy.dma_start(out=dbg5[:, 0:T], in_=bcA1_sb[:, :]).then_inc(d_dbg, 16); n += 1
                    sy.wait_ge(d_dbg, 16 * (n - 1))
                else:
                    sy.wait_ge(v_quads, 4)
                    sy.wait_ge(v_lkT, 16)
        ps_qk.close(); es0.close(); es01.close(); es012.close(); es.close()
        return nc

    es0.close()
    es1 = contextlib.ExitStack()
    vz_sb = [mk(es1, f"vz_sb{i}", [128, 2 * E], BF16) for i in range(2)]
    pz_sb = [mk(es1, f"pz_sb{i}", [128, E], BF16) for i in range(2)]
    L_sb = mk(es1, "L_sb", [128, E], BF16)
    prun_sb = mk(es1, "prun_sb", [128, E])
    mtile_sb = mk(es1, "mtile_sb", [128, 2 * 512])


    ps_qk.close()
    ps1 = contextlib.ExitStack()
    pv = [ps1.enter_context(nc.psum_tensor(f"pv{i}", [128, 1024], F32)) for i in range(2)]
    linkv_ps = ps1.enter_context(nc.psum_tensor("linkv_ps", [128, E], F32))

    with nc.Block() as blk:
        @blk.gpsimd
        def _(g_):
            g_.dma_start(
                out=bass.AP(whg0_sb.ap().tensor, 0,
                            [[FCH * 1024, 128], [1024, FCH], [1, 1024]]),
                in_=chunkT(whga_d, FCH, 1024),
            ).then_inc(d_whg[0], 16)

        @blk.sync
        def _(sy):
            if TRUNC == 0 or TRUNC >= 4:
                sy.dma_start(out=mtile_sb[:, 0:512], in_=neg_d[:, :]).then_inc(d_msk[0], 16)
                for b in range(32):
                    j = b + 1
                    if j >= 2:
                        sy.wait_ge(v_msk, j - 1)
                    sy.dma_start(out=mtile_sb[:, (j % 2) * 512:(j % 2 + 1) * 512],
                                 in_=masks_d[b, :, :]).then_inc(d_msk[j % 2], 16)

        @blk.vector
        def _(v):
            if TRUNC == 0 or TRUNC >= 4:
                v.wait_ge(d_msk[0], 16)
                v.tensor_copy(biasT_sb[:, :], mtile_sb[:, 0:512]).then_inc(v_msk, 1)
                for b in range(32):
                    j = b + 1
                    v.wait_ge(d_msk[j % 2], 16 * (j // 2 + 1))
                    v.scalar_tensor_tensor(biasT_sb[:, :],
                                           mtile_sb[:, (j % 2) * 512:(j % 2 + 1) * 512],
                                           tcol_sb[:, b:b + 1], biasT_sb[:, :],
                                           ALU.mult, ALU.add).then_inc(v_msk, 1)

        @blk.vector
        def _(v):
            v.memset(pz_sb[0][:, :], 0.0).then_inc(s_snap, 1)

        @blk.tensor
        def _(t):
            t.wait_ge(s_qk, 1)
            t.wait_ge(v_lkT, 16)
            t.wait_ge(v_tcol, 1)
            for g in range(NG):
                for r in range(4):
                    tc, eh = divmod(r, 2)
                    if g == 0 and r <= 1:
                        t.wait_ge(d_whv[eh], 16)
                    rr = 4 * g + r
                    if rr >= 2:
                        t.wait_ge(s_vsil, rr - 1)
                    pv_ = pv[r % 2]
                    for ch in range(FCH):
                        for sl in range(2):
                            t.matmul(
                                pv_[:, sl * 512:(sl + 1) * 512],
                                nsc_sb[:, ch * T + g * G + tc * 128:
                                       ch * T + g * G + tc * 128 + 128],
                                whv_sb[:, ch * E + eh * 1024 + sl * 512:
                                       ch * E + eh * 1024 + (sl + 1) * 512],
                                start=(ch == 0), stop=False)
                    for sl in range(2):
                        ins = t.matmul(pv_[:, sl * 512:(sl + 1) * 512], ones1b[:],
                                       bhv_sb[0:1, eh * 1024 + sl * 512:
                                              eh * 1024 + (sl + 1) * 512],
                                       start=False, stop=True)
                    ins.then_inc(t_v, 1)
                t.wait_ge(s_vsil, 4 * (g + 1))
                if g >= 1:
                    t.wait_ge(v_lkv, g)
                for jc in range(2):
                    for ns in range(4):
                        ins = t.matmul(
                            linkv_ps[:, ns * 512:(ns + 1) * 512],
                            lkt_sb[:, (2 * g + jc) * 128:(2 * g + jc) * 128 + 128],
                            vz_sb[g % 2][:, jc * E + ns * 512: jc * E + (ns + 1) * 512],
                            start=(jc == 0), stop=(jc == 1))
                ins.then_inc(t_lin, 1)

        @blk.scalar
        def _(s):
            for g in range(NG):
                for r in range(4):
                    tc, eh = divmod(r, 2)
                    s.wait_ge(t_v, 4 * g + r + 1)
                    if g >= 2 and r == 0:
                        s.wait_ge(d_vw[g % 2], 16 * (g // 2))
                    s.activation(vz_sb[g % 2][:, tc * E + eh * 1024:
                                              tc * E + (eh + 1) * 1024],
                                 pv[r % 2][:, :], AF.Silu).then_inc(s_vsil, 1)

        @blk.vector
        def _(v):
            for g in range(NG):
                v.wait_ge(t_lin, g + 1)
                if g == 0:
                    v.tensor_copy(prun_sb[:, :], linkv_ps[:, :]).then_inc(v_lkv, 1)
                else:
                    v.tensor_add(prun_sb[:, :], prun_sb[:, :],
                                 linkv_ps[:, :]).then_inc(v_lkv, 1)
                if g < NG - 1:
                    if g + 1 >= 2:
                        v.wait_ge(d_pw[(g + 1) % 2], 16 * ((g + 1) // 2))
                    v.tensor_scalar(pz_sb[(g + 1) % 2][:, :], prun_sb[:, :], 1.0 / G,
                                    None, ALU.mult).then_inc(s_snap, 1)
            v.tensor_scalar(L_sb[:, :], prun_sb[:, :], 1.0 / G,
                            None, ALU.mult).then_inc(s_L, 1)

        @blk.sync
        def _(sy):
            for g in range(NG):
                sy.wait_ge(s_vsil, 4 * (g + 1))
                sy.dma_start(out=v_sp[g, :, :],
                             in_=vz_sb[g % 2][:, :]).then_inc(d_vw[g % 2], 16)
                sy.wait_ge(s_snap, g + 1)
                sy.dma_start(out=p_sp[g, :, :],
                             in_=pz_sb[g % 2][:, :]).then_inc(d_pw[g % 2], 16)
            sy.wait_ge(t_v, 4 * NG)
            sy.dma_start(out=nsc_sp[:, :], in_=nsc_sb[:, :]).then_inc(d_nw, 16)


    if stage == 1:
        gathd = [mk(es1, f"gathd{i}", [128, E], BF16) for i in range(2)]
        Sd = mk(es1, "Sd", [128, E])
        with nc.Block() as blk:
            @blk.gpsimd
            def _(g_):
                g_.wait_ge(s_L, 1)
                g_.dma_start(out=cc_in[:, :], in_=L_sb[:, :]).then_inc(d_ci, 16)
                g_.wait_ge(d_ci, 16)
                g_.collective_compute(
                    "AllGather", ALU.bypass,
                    replica_groups=[list(range(NCORES))],
                    ins=[cc_in.ap().opt()],
                    outs=[cc_out.ap().opt()],
                ).then_inc(cc_done, 1)
                g_.wait_ge(cc_done, 2)
                for i in range(NCORES):
                    if i >= 2:
                        g_.wait_ge(v_Ss, i - 1)
                    g_.dma_start(out=gathd[i % 2][:, :],
                                 in_=cc_out[i * 128:(i + 1) * 128, :]
                                 ).then_inc(d_co[i % 2], 16)

            @blk.vector
            def _(v):
                v.memset(Sd[:, :], 0.0)
                for i in range(NCORES):
                    v.wait_ge(d_co[i % 2], 16 * (i // 2 + 1))
                    v.scalar_tensor_tensor(Sd[:, :], gathd[i % 2][:, :],
                                           pmask_sb[:, i:i + 1], Sd[:, :],
                                           ALU.mult, ALU.add).then_inc(v_Ss, 1)

            @blk.sync
            def _(sy):
                if debug:
                    sy.wait_ge(v_Ss, NCORES)
                    sy.dma_start(out=dbg5[:, :], in_=Sd[:, :]).then_inc(d_dbg, 16)
                    sy.wait_ge(d_dbg, 16)
                for dd in (d_vw, d_pw):
                    sy.wait_ge(dd[0], 16 * NG // 2)
                    sy.wait_ge(dd[1], 16 * NG // 2)
                sy.wait_ge(d_nw, 16)
                sy.wait_ge(v_Ss, NCORES)
        es1.close(); es01.close(); ps1.close(); es012.close(); es.close()
        return nc

    es1.close(); es01.close()
    es2 = contextlib.ExitStack()
    whg1_sb = mk(es2, "whg1_sb", [128, FCH * 1024], BF16)
    wout_sb = mk(es2, "wout_sb", [128, ECH * D], BF16)
    S_f32 = mk(es2, "S_f32", [128, E // 2])
    gath_sb = [mk(es2, f"gath_sb{i}", [128, E // 2], BF16) for i in range(2)]
    v2_sb = [mk(es2, f"v2_sb{i}", [128, 2 * E], BF16) for i in range(2)]
    p2_sb = [mk(es2, f"p2_sb{i}", [128, E], BF16) for i in range(2)]
    n2_sb = [mk(es2, f"n2_sb{i}", [128, FCH * G], BF16) for i in range(2)]
    x2_sb = [mk(es2, f"x2_sb{i}", [128, OCH * G]) for i in range(2)]
    hid_sb = [mk(es2, f"hid_sb{i}", [128, ECH * G], BF16) for i in range(2)]
    sims_sb = [mk(es2, f"sims_sb{i}", [128, 512]) for i in range(2)]
    relu_sb = mk(es2, "relu_sb", [128, 512], BF16)
    xb3_sb = mk(es2, "xb3_sb", [128, T], BF16)
    atn_sb = [mk(es2, f"atn_sb{i}", [128, 512], BF16) for i in range(NG)]

    ps1.close()
    ps2 = contextlib.ExitStack()
    sim_ps = ps2.enter_context(nc.psum_tensor("sim_ps", [128, 512], F32))
    gate_ps = [ps2.enter_context(nc.psum_tensor(f"gate_ps{i}", [128, G], F32))
               for i in range(2)]
    b_ps = [ps2.enter_context(nc.psum_tensor(f"b_ps{i}", [128, G], F32)) for i in range(2)]
    wo_ps = [ps2.enter_context(nc.psum_tensor(f"wo_ps{i}", [128, 512], F32))
             for i in range(2)]
    st3_ps = ps2.enter_context(nc.psum_tensor("st3_ps", [128, 512], F32))

    P1GATES = [(d_nw, 16), (d_vw[0], 16 * NG // 2), (d_vw[1], 16 * NG // 2),
               (d_pw[0], 16 * NG // 2), (d_pw[1], 16 * NG // 2),
               (d_ci, 16)]

    with nc.Block() as blk:
        @blk.gpsimd
        def _(g_):
            g_.wait_ge(s_L, 1)
            g_.dma_start(out=cc_in[:, :], in_=L_sb[:, :]).then_inc(d_ci, 16)
            g_.wait_ge(d_ci, 16)
            g_.collective_compute(
                "AllGather", ALU.bypass,
                replica_groups=[list(range(NCORES))],
                ins=[cc_in.ap().opt()],
                outs=[cc_out.ap().opt()],
            ).then_inc(cc_done, 1)
            for s_, v_ in P1GATES:
                g_.wait_ge(s_, v_)
            g_.dma_start(
                out=bass.AP(whg1_sb.ap().tensor, 0,
                            [[FCH * 1024, 128], [1024, FCH], [1, 1024]]),
                in_=chunkT(whgb_d, FCH, 1024),
            ).then_inc(d_whg[1], 16)
            g_.wait_ge(cc_done, 2)
            for j in range(2 * NCORES):
                h, i = divmod(j, NCORES)
                if j >= 2:
                    g_.wait_ge(v_Ss, j - 1 + (1 if (j - 2) >= NCORES else 0))
                g_.dma_start(
                    out=gath_sb[j % 2][:, :],
                    in_=cc_out[i * 128:(i + 1) * 128,
                               h * (E // 2):(h + 1) * (E // 2)],
                ).then_inc(d_co[j % 2], 16)
            for i in range(4):
                g_.dma_start(
                    out=wout_sb[:, i * 4 * D:(i + 1) * 4 * D],
                    in_=chunkT(wout_d, ECH, D, ch0=4 * i, nsub=4),
                ).then_inc(d_wout, 16)

        @blk.sync
        def _(sy):
            for s_, v_ in P1GATES:
                sy.wait_ge(s_, v_)
            for g in range(NG):
                if g >= 2:
                    sy.wait_ge(t_Bg, g - 1)
                    sy.wait_ge(v_x1, 4 * (g - 1))
                sy.wait_ge(d_vw[g % 2], 16 * (g // 2 + 1))
                sy.wait_ge(d_pw[g % 2], 16 * (g // 2 + 1))
                sy.dma_start(out=v2_sb[g % 2][:, :],
                             in_=v_sp[g, :, :]).then_inc(d_vr[g % 2], 16)
                sy.dma_start(out=p2_sb[g % 2][:, :],
                             in_=p_sp[g, :, :]).then_inc(d_pr[g % 2], 16)
                sy.dma_start(
                    out=n2_sb[g % 2][:, :],
                    in_=bass.AP(nsc_sp.tensor, g * G,
                                [[FCH * T, 128], [T, FCH], [1, G]]),
                ).then_inc(d_nr[g % 2], 16)
                if g >= 2:
                    sy.wait_ge(d_x1w[g % 2], 16 * (g // 2))
                    sy.wait_ge(s_sq3, 2 * g - 2)
                sy.dma_start(
                    out=x2_sb[g % 2][:, :],
                    in_=bass.AP(x0t.tensor, g * G,
                                [[T, 128], [128 * T, OCH], [1, G]]),
                ).then_inc(d_x0g[g % 2], 16)
                if g >= 1:
                    sy.wait_ge(v_x1, 4 * g)
                    sy.dma_start(out=x1_sp[g - 1, :, :],
                                 in_=x2_sb[(g - 1) % 2][:, :]
                                 ).then_inc(d_x1w[(g - 1) % 2], 16)
            sy.wait_ge(v_x1, 4 * NG)
            sy.dma_start(out=x1_sp[NG - 1, :, :],
                         in_=x2_sb[(NG - 1) % 2][:, :]
                         ).then_inc(d_x1w[(NG - 1) % 2], 16)

        @blk.tensor
        def _(t):
            for g in range(NG):
                t.wait_ge(v_sim1, g)
                for jc in range(2):
                    ins = t.matmul(sim_ps[:, jc * 256:(jc + 1) * 256],
                                   qkk_sb[:, g * G + jc * 128: g * G + jc * 128 + 128],
                                   qq_sb[:, g * G:(g + 1) * G], start=True, stop=True)
                ins.then_inc(t_sim, 1)
            def gates(t, g, m):
                idx = ECH * g + m
                if g == 0 and m in (0, 8):
                    t.wait_ge(d_whg[m // 8], 16)
                if idx >= 2:
                    t.wait_ge(s_gate, idx - 1)
                gp = gate_ps[m % 2]
                whgh = whg0_sb if m < 8 else whg1_sb
                for ch in range(FCH):
                    ins = t.matmul(gp[:, :],
                                   whgh[:, ch * 1024 + (m % 8) * 128:
                                        ch * 1024 + (m % 8) * 128 + 128],
                                   n2_sb[g % 2][:, ch * G:(ch + 1) * G],
                                   start=(ch == 0), stop=(ch == FCH - 1))
                ins.then_inc(t_gate, 1)
            def x1stats(t, sg):
                t.wait_ge(s_sq3, 2 * sg + 1)
                if sg >= 1:
                    t.wait_ge(d_x1b[0], sg)
                for o in range(OCH):
                    ins = t.matmul(st3_ps[:, 0:G], onesmatb[:],
                                   xb3_sb[:, o * G:(o + 1) * G],
                                   start=(o == 0), stop=(o == OCH - 1))
                ins.then_inc(t_st3, 1)
                t.wait_ge(s_sq3, 2 * sg + 2)
                for o in range(OCH):
                    ins = t.matmul(st3_ps[:, G:2 * G], onesmatb[:],
                                   xb3_sb[:, o * G:(o + 1) * G],
                                   start=(o == 0), stop=(o == OCH - 1))
                ins.then_inc(t_st3, 1)
            for g in range(2):
                t.wait_ge(d_nr[g % 2], 16)
                for m in range(ECH):
                    gates(t, g, m)
            for g in range(NG):
                t.wait_ge(d_vr[g % 2], 16 * (g // 2 + 1))
                t.wait_ge(d_pr[g % 2], 16 * (g // 2 + 1))
                if g >= 2:
                    t.wait_ge(d_nr[g % 2], 16 * (g // 2 + 1))
                if g == 0:
                    t.wait_ge(v_Ss, 2 * NCORES + 2)
                for m in range(ECH):
                    idx = ECH * g + m
                    if g >= 2:
                        gates(t, g, m)
                    if idx >= 2:
                        t.wait_ge(v_hid, idx - 1)
                    if m == 0:
                        t.wait_ge(v_atn, g + 1)
                    bp = b_ps[m % 2]
                    for jc in range(2):
                        t.matmul(bp[:, :],
                                 v2_sb[g % 2][:, jc * E + m * 128: jc * E + m * 128 + 128],
                                 atn_sb[g][:, jc * 256:(jc + 1) * 256],
                                 start=(jc == 0), stop=False)
                    t.matmul(bp[:, :], p2_sb[g % 2][:, m * 128:(m + 1) * 128],
                             lq_sb[:, g * G:(g + 1) * G], start=False, stop=False)
                    t.matmul(bp[:, :], S_sb[:, m * 128:(m + 1) * 128],
                             lq_sb[:, g * G:(g + 1) * G],
                             start=False, stop=True).then_inc(t_B, 1)
                t.nop().then_inc(t_Bg, 1)
                if g >= 1:
                    x1stats(t, g - 1)
                t.wait_ge(v_hid, ECH * (g + 1))
                t.wait_ge(d_wout, 64)
                for p_ in range(4):
                    pi = 4 * g + p_
                    if pi >= 2:
                        t.wait_ge(v_x1, pi - 1)
                    wp = wo_ps[p_ % 2]
                    for half in range(2):
                        o = 2 * p_ + half
                        for ek in range(ECH):
                            t.matmul(wp[:, half * 256:(half + 1) * 256],
                                     wout_sb[:, ek * D + o * 128: ek * D + o * 128 + 128],
                                     hid_sb[g % 2][:, ek * G:(ek + 1) * G],
                                     start=(ek == 0), stop=False)
                        t.matmul(wp[:, half * 256:(half + 1) * 256],
                                 bout_sb[0:1, o * 128:(o + 1) * 128],
                                 onesrowb[0:1, :], start=False, stop=False)
                        ins = t.matmul(wp[:, half * 256:(half + 1) * 256], ones1b[:],
                                       negb_sb[0:1, g * G:(g + 1) * G],
                                       start=False, stop=True)
                    ins.then_inc(t_wout, 1)
            x1stats(t, NG - 1)

        @blk.scalar
        def _(s):
            for g in range(NG):
                s.wait_ge(v_sim1, g + 1)
                s.wait_ge(v_atn, g)
                s.activation(relu_sb[:, :], sims_sb[g % 2][:, :], AF.Relu).then_inc(s_relu, 1)
            def x1cast(s, cg):
                s.wait_ge(v_x1, 4 * (cg + 1))
                if cg >= 1:
                    s.wait_ge(t_st3, 2 * cg)
                s.activation(xb3_sb[:, :], x2_sb[cg % 2][:, :],
                             AF.Copy).then_inc(s_sq3, 1)
                s.wait_ge(t_st3, 2 * cg + 1)
                s.activation(xb3_sb[:, :], x2_sb[cg % 2][:, :],
                             AF.Square).then_inc(s_sq3, 1)
            for g in range(NG):
                for m in range(ECH):
                    idx = ECH * g + m
                    s.wait_ge(t_gate, idx + 1)
                    s.activation(hid_sb[g % 2][:, m * G:(m + 1) * G],
                                 gate_ps[m % 2][:, :], AF.Silu,
                                 bias=bhg_sb[:, m:m + 1]).then_inc(s_gate, 1)
                if g >= 1:
                    x1cast(s, g - 1)
            x1cast(s, NG - 1)

        @blk.vector
        def _(v):
            for g in range(NG):
                v.wait_ge(t_sim, g + 1)
                v.wait_ge(v_msk, 33)
                if g >= 2:
                    v.wait_ge(s_relu, g - 1)
                    v.wait_ge(v_atn, g - 1)
                v.tensor_add(sims_sb[g % 2][:, :], sim_ps[:, :],
                             biasT_sb[:, :]).then_inc(v_sim1, 1)
                v.wait_ge(s_relu, g + 1)
                v.tensor_mul(atn_sb[g][:, :], relu_sb[:, :],
                             sims_sb[g % 2][:, :]).then_inc(v_atn, 1)
            for h in range(2):
                v.memset(S_f32[:, :], 0.0)
                for i in range(NCORES):
                    j = h * NCORES + i
                    v.wait_ge(d_co[j % 2], 16 * (j // 2 + 1))
                    v.scalar_tensor_tensor(S_f32[:, :], gath_sb[j % 2][:, :],
                                           pmask_sb[:, i:i + 1], S_f32[:, :],
                                           ALU.mult, ALU.add).then_inc(v_Ss, 1)
                v.tensor_copy(S_sb[:, h * (E // 2):(h + 1) * (E // 2)],
                              S_f32[:, :]).then_inc(v_Ss, 1)
            for g in range(NG):
                if g >= 2:
                    v.wait_ge(t_wout, 4 * (g - 1))
                for m in range(ECH):
                    idx = ECH * g + m
                    v.wait_ge(t_B, idx + 1)
                    v.wait_ge(s_gate, idx + 1)
                    hsl = hid_sb[g % 2][:, m * G:(m + 1) * G]
                    v.tensor_mul(hsl, hsl, b_ps[m % 2][:, :]).then_inc(v_hid, 1)
                v.wait_ge(v_bc, 1)
                v.wait_ge(d_x0g[g % 2], 16 * (g // 2 + 1))
                if g >= 2:
                    v.wait_ge(d_x1w[g % 2], 16 * (g // 2))
                for p_ in range(4):
                    v.wait_ge(t_wout, 4 * g + p_ + 1)
                    for half in range(2):
                        o = 2 * p_ + half
                        dst = x2_sb[g % 2][:, o * G:(o + 1) * G]
                        v.tensor_mul(dst, dst,
                                     bcA1_sb[:, g * G:(g + 1) * G])
                        ins = v.tensor_add(
                            dst, dst, wo_ps[p_ % 2][:, half * 256:(half + 1) * 256])
                    ins.then_inc(v_x1, 1)
                if g >= 1:
                    v.wait_ge(t_st3, 2 * g)
                    v.tensor_copy(q_sum[:, (g - 1) * G:g * G], st3_ps[:, 0:G])
                    v.tensor_copy(q_sq[:, (g - 1) * G:g * G],
                                  st3_ps[:, G:2 * G]).then_inc(d_x1b[0], 1)
            v.wait_ge(t_st3, 2 * NG)
            v.tensor_copy(q_sum[:, (NG - 1) * G:NG * G], st3_ps[:, 0:G])
            v.tensor_copy(q_sq[:, (NG - 1) * G:NG * G],
                          st3_ps[:, G:2 * G]).then_inc(d_x1b[0], 1)

    if stage == 2:
        with nc.Block() as blk:
            @blk.sync
            def _(sy):
                sy.wait_ge(d_x1w[0], 16 * NG // 2)
                sy.wait_ge(d_x1w[1], 16 * NG // 2)
                if debug:
                    sy.dma_start(out=dbg1[:, :], in_=bcA1_sb[:, :]).then_inc(d_dbg, 16)
                    sy.dma_start(out=dbg5[:, :], in_=x2_sb[1][:, :]).then_inc(d_dbg, 16)
                    sy.dma_start(out=dbg4[:, 0:ECH * G], in_=hid_sb[1][:, :]).then_inc(d_dbg, 16)
                    sy.dma_start(out=dbg4[:, ECH * G:ECH * G + 512],
                                 in_=atn_sb[1][:, :]).then_inc(d_dbg, 16)
                    sy.dma_start(out=dbg2[:, :], in_=sims_sb[1][:, :]).then_inc(d_dbg, 16)
                    sy.dma_start(out=dbg4[:, 4608:6656],
                                 in_=wout_sb[:, 0:2048]).then_inc(d_dbg, 16)
                    sy.wait_ge(d_dbg, 96)
        es2.close(); ps2.close(); es012.close(); es.close()
        return nc

    es2.close(); es012.close()
    es3 = contextlib.ExitStack()
    n3_sb = mk(es3, "n3_sb", [128, KCH * HT], BF16)
    d_sb = mk(es3, "d_sb", [128, 32 * 512], BF16)
    bc3a_sb = mk(es3, "bc3a_sb", [128, T])
    bc3b_sb = mk(es3, "bc3b_sb", [128, T])
    es3a = contextlib.ExitStack()
    q0 = mk(es3a, "q0", [128, T]); q1 = mk(es3a, "q1", [128, T])
    q2 = mk(es3a, "q2", [128, T])
    mk_sb = mk(es3a, "mk_sb", [128, 32 * 128], BF16)
    up_sb = mk(es3a, "up_sb", [128, 32 * 512], BF16)

    ps2.close()
    ps3 = contextlib.ExitStack()
    z_ps = [ps3.enter_context(nc.psum_tensor(f"z_ps{i}", [128, 512], F32)) for i in range(4)]
    cp_ps = [ps3.enter_context(nc.psum_tensor(f"cp_ps{i}", [128, 512], F32))
             for i in range(4)]

    P2GATES = [(d_x1w[0], 16 * NG // 2), (d_x1w[1], 16 * NG // 2),
               (v_x1, 4 * NG), (s_gate, ECH * NG),
               (t_Bg, NG), (v_hid, ECH * NG), (v_atn, NG), (t_wout, 4 * NG),
               (t_st3, 2 * NG), (d_x1b[0], NG)]

    with nc.Block() as blk:
        @blk.gpsimd
        def _(g2):
            for s_, v_ in P2GATES:
                g2.wait_ge(s_, v_)
            g2.dma_start(out=mk_sb[:, :],
                         in_=bass.AP(makerbd_d.tensor, 0,
                                     [[128, 128], [128 * 128, 32], [1, 128]])
                         ).then_inc(d_mk, 16)
            g2.dma_start(out=up_sb[:, :],
                         in_=bass.AP(upbd_d.tensor, 0,
                                     [[512, 128], [128 * 512, 32], [1, 512]])
                         ).then_inc(d_up[0], 16)

        @blk.scalar
        def _(s):
            s.wait_ge(v_r31, 1)
            s.activation(q1[:, :], q2[:, :], AF.Sqrt).then_inc(s_r31, 1)

        @blk.vector
        def _(v):
            v.tensor_scalar(q0[:, :], q_sum[:, :], 1.0 / D, None, ALU.mult)
            v.tensor_scalar(q1[:, :], q_sq[:, :], 1.0 / D, None, ALU.mult)
            v.tensor_mul(q2[:, :], q0[:, :], q0[:, :])
            v.tensor_sub(q1[:, :], q1[:, :], q2[:, :])
            v.tensor_scalar(q1[:, :], q1[:, :], EPS, None, ALU.add)
            v.reciprocal(q2[:, :], q1[:, :]).then_inc(v_r31, 1)
            v.wait_ge(s_r31, 1)
            v.tensor_mul(q0[:, :], q0[:, :], q1[:, :]).then_inc(v_rows3, 1)
            v.tensor_copy(bc3a_sb[:, :], q1[:, :]).then_inc(v_bc3, 1)
            v.tensor_copy(bc3b_sb[:, :], q0[:, :]).then_inc(v_bc3, 1)
            for i in range(32):
                v.wait_ge(t_D, i + 1)
                v.tensor_copy(d_sb[:, i * 512:(i + 1) * 512], z_ps[i % 2][:, :]).then_inc(v_D, 1)

        @blk.tensor
        def _(t):
            t.wait_ge(d_mk, 16)
            t.wait_ge(d_up[0], 16)
            for i in range(32):
                if i >= 2:
                    t.wait_ge(v_D, i - 1)
                t.matmul(z_ps[i % 2][:, :], mk_sb[:, i * 128:(i + 1) * 128],
                         up_sb[:, i * 512:(i + 1) * 512],
                         start=True, stop=True).then_inc(t_D, 1)

    es3a.close()
    es3b = contextlib.ExitStack()
    h_sb = mk(es3b, "h_sb", [128, KCH * HT], BF16)
    w3_sb = [mk(es3b, f"w3_sb{i}", [128, KCH * D], BF16) for i in range(2)]
    oacc_sb = mk(es3b, "oacc_sb", [128, OCH * HT])
    x1h_sb = [mk(es3b, f"x1h_sb{i}", [128, HT]) for i in range(2)]
    nt3_sb = mk(es3b, "nt3_sb", [128, HT])

    ES3A_GATES = [(t_st3, 2 * NG), (s_sq3, 2 * NG), (v_bc3, 2), (t_D, 32), (v_D, 32),
                  (v_rows3, 1)]

    with nc.Block() as blk:
        @blk.sync
        def _(sy):
            for s_, v_ in ES3A_GATES:
                sy.wait_ge(s_, v_)
            slot_evt = [None, None]
            slot_i = 0
            def x1h_load(hf, ch, evt):
                nonlocal slot_i
                sl_ = slot_i % 2
                if slot_evt[sl_] is not None:
                    sy.wait_ge(*slot_evt[sl_])
                sy.dma_start(
                    out=x1h_sb[sl_][:, :],
                    in_=bass.AP(x1_sp.tensor, hf * 4 * 128 * OCH * G + ch * G,
                                [[OCH * G, 128], [128 * OCH * G, 4], [1, G]]),
                ).then_inc(d_x1h[sl_], 16)
                slot_evt[sl_] = evt
                slot_i += 1
            for hf in range(NHF):
                for ch in range(OCH):
                    x1h_load(hf, ch, (v_n3, OCH * hf + ch + 1))
                for ch in range(OCH):
                    x1h_load(hf, ch, (v_out, OCH * hf + ch + 1))
                sy.wait_ge(v_out, OCH * (hf + 1))
                sy.dma_start(
                    out=bass.AP(out_d.tensor, hf * HT,
                                [[T, 128], [128 * T, OCH], [1, HT]]),
                    in_=oacc_sb[:, :],
                ).then_inc(d_out, 16)
            sy.wait_ge(d_out, 32)

        @blk.gpsimd
        def _(g_):
            for s_, v_ in ES3A_GATES:
                g_.wait_ge(s_, v_)
            for ks in range(NKS):
                if ks >= 2:
                    g_.wait_ge(t_cp, CPS * (ks - 1))
                sl = ks % NSL
                g_.dma_start(
                    out=w3_sb[ks % 2][:, :],
                    in_=chunkT(wct_d, FF // 128, D, ch0=sl * KCH, nsub=KCH),
                ).then_inc(d_w3[ks % 2], 16)

        @blk.tensor
        def _(t):
            zi = 0
            for ks in range(NKS):
                hf, sl = divmod(ks, NSL)
                t.wait_ge(v_n3, OCH * hf + min(2 * sl + 2, OCH))
                t.wait_ge(d_w3[ks % 2], 16 * (ks // 2 + 1))

                def zburst(nt):
                    nonlocal zi
                    for st in range(STR):
                        gst = sl * STR + st
                        for m in range(4):
                            if zi >= 4:
                                t.wait_ge(s_z, zi - 3)
                            t.matmul(z_ps[zi % 4][:, :],
                                     d_sb[:, gst * 512 + m * 128: gst * 512 + (m + 1) * 128],
                                     n3_sb[:, (gst % KCH) * HT + nt * 512:
                                           (gst % KCH) * HT + (nt + 1) * 512],
                                     start=True, stop=True).then_inc(t_z, 1)
                            zi += 1

                def cpgroup(qi, oh, nt):
                    for ot in range(4):
                        cpi = CPS * ks + qi * 4 + ot
                        if cpi >= 4:
                            t.wait_ge(v_acc, cpi - 3)
                        for k in range(KCH):
                            t.wait_ge(s_z, ZPS * ks + nt * 8 + k + 1)
                            ins = t.matmul(
                                cp_ps[ot][:, :],
                                w3_sb[ks % 2][:, k * D + oh * 512 + ot * 128:
                                              k * D + oh * 512 + (ot + 1) * 128],
                                h_sb[:, k * HT + nt * 512: k * HT + (nt + 1) * 512],
                                start=(k == 0), stop=(k == KCH - 1))
                        ins.then_inc(t_cp, 1)

                zburst(0)
                cpgroup(0, 0, 0)
                zburst(1)
                cpgroup(1, 1, 0)
                cpgroup(2, 0, 1)
                cpgroup(3, 1, 1)

        @blk.scalar
        def _(s):
            zi = 0
            for ks in range(NKS):
                if ks >= 1:
                    s.wait_ge(t_cp, CPS * ks)
                for nt in range(2):
                    for st in range(STR):
                        for m in range(4):
                            fk = st * 4 + m
                            s.wait_ge(t_z, zi + 1)
                            s.activation(h_sb[:, fk * HT + nt * 512:
                                              fk * HT + (nt + 1) * 512],
                                         z_ps[zi % 4][:, :], AF.Gelu).then_inc(s_z, 1)
                            zi += 1

        @blk.vector
        def _(v):
            for hf in range(NHF):
                for ch in range(OCH):
                    li = 2 * OCH * hf + ch
                    v.wait_ge(d_x1h[li % 2], 16 * (li // 2 + 1))
                    v.tensor_mul(nt3_sb[:, :], x1h_sb[ch % 2][:, :],
                                 bc3a_sb[:, hf * HT:(hf + 1) * HT])
                    v.tensor_sub(n3_sb[:, ch * HT:(ch + 1) * HT], nt3_sb[:, :],
                                 bc3b_sb[:, hf * HT:(hf + 1) * HT]).then_inc(v_n3, 1)
                if hf >= 1:
                    v.wait_ge(d_out, 16 * hf)
                for sl in range(NSL):
                    ks = hf * NSL + sl
                    for qi, (oh, nt) in enumerate([(0, 0), (1, 0), (0, 1), (1, 1)]):
                        for ot in range(4):
                            cpi = CPS * ks + qi * 4 + ot
                            v.wait_ge(t_cp, cpi + 1)
                            dst = oacc_sb[:, (oh * 4 + ot) * HT + nt * 512:
                                          (oh * 4 + ot) * HT + (nt + 1) * 512]
                            if sl == 0:
                                v.tensor_copy(dst, cp_ps[ot][:, :]).then_inc(v_acc, 1)
                            else:
                                v.tensor_add(dst, dst,
                                             cp_ps[ot][:, :]).then_inc(v_acc, 1)
                for ch in range(OCH):
                    li = 2 * OCH * hf + OCH + ch
                    v.wait_ge(d_x1h[li % 2], 16 * (li // 2 + 1))
                    v.tensor_add(oacc_sb[:, ch * HT:(ch + 1) * HT],
                                 oacc_sb[:, ch * HT:(ch + 1) * HT], x1h_sb[ch % 2][:, :]).then_inc(v_out, 1)

    es3b.close(); es3.close(); ps3.close(); es.close()
    return nc


def kernel(**inputs):
    in_maps = prep_inputs(inputs)
    nc = build_nc()
    res = run_bass_kernel_spmd(nc, in_maps, core_ids=list(range(NCORES)))
    shards = [np.ascontiguousarray(res.results[c]["out"].T) for c in range(NCORES)]
    return np.concatenate(shards, axis=0).reshape(1, 16384, 1024).astype(np.float32)


# version 16
# speedup vs baseline: 1.0174x; 1.0174x over previous
import contextlib
import numpy as np
import ml_dtypes
import concourse.bass as bass
import concourse.mybir as mybir
from concourse.bass_utils import run_bass_kernel_spmd

F32 = mybir.dt.float32
BF16 = mybir.dt.bfloat16
AF = mybir.ActivationFunctionType
ALU = mybir.AluOpType

D = 1024
FCH = 8
T = 2048
G = 256
NG = 8
E = 2048
ECH = 16
QKD = 128
FF = 16384
OCH = 8
EPS = 1e-5
NEGBIG = -1.0e9
BIAS_SCALE = float(np.sqrt(128.0))
NCORES = 8

NHF = 2
HT = 1024
SLF = 1024
NSL = FF // SLF
KCH = SLF // 128
STR = SLF // 512
ZPS = STR * 4 * 2
CPS = 2 * 2 * 4
NKS = NHF * NSL

import os
STAGE = int(os.environ.get("MK_STAGE", "3"))
DEBUG = os.environ.get("MK_DEBUG", "0") == "1"
TRUNC = int(os.environ.get("MK_TRUNC", "0"))


def _bucket_of_delta(delta: int) -> int:
    n = max(delta, 0)
    if n < 16:
        return n
    val = 16 + int(np.log(max(n, 1) / 16.0) / np.log(128.0 / 16.0) * 16.0)
    return min(val, 31)


def host_constants():
    M = np.zeros((32, 128, 512), dtype=np.float32)
    NEG = np.zeros((128, 512), dtype=np.float32)
    for jc in range(2):
        for j in range(128):
            for i in range(256):
                d_ = i - j - 128 * jc
                if d_ < 0:
                    NEG[j, jc * 256 + i] = NEGBIG
                else:
                    M[_bucket_of_delta(d_), j, jc * 256 + i] = 1.0
    return M, NEG, np.eye(128, dtype=np.float32)


def prep_inputs(inputs):
    a = {k: np.asarray(v) for k, v in inputs.items()}
    x = a["L0_attn_x"][0]
    Wh = a["L0_attn_Wh"]; Wqk = a["L0_attn_Wqk"]; Wout = a["L0_attn_Wout"]
    bh = a["L0_attn_bh"]; bqk = a["L0_attn_bqk"]; bout = a["L0_attn_bout"]
    qk_w = a["L0_attn_qk_w"]; qk_b = a["L0_attn_qk_b"]
    lnw = a["L0_attn_ln_w"]; lnb = a["L0_attn_ln_b"]
    rel = a["L0_attn_rel_table"]
    maker = a["L0_ffn_rail_maker"]; up = a["L0_ffn_rail_up_proj"]; wc = a["L0_ffn_w_cproj"]

    M, NEG, ident = host_constants()
    makerbd = np.zeros((32, 128, 128), dtype=np.float32)
    upbd = np.zeros((32, 128, 512), dtype=np.float32)
    for h in range(4):
        for cc in range(8):
            idx = h * 8 + cc
            for b in range(8):
                r = 8 * cc + b
                makerbd[idx, 16 * b:16 * b + 16, 16 * b:16 * b + 16] = maker[h, r].T
                upbd[idx, 16 * b:16 * b + 16, 64 * b:64 * b + 64] = up[h, r].T
    common = {
        "whva": np.ascontiguousarray(Wh[:, 0:1024]),
        "whvb": np.ascontiguousarray(Wh[:, 1024:2048]),
        "whga": np.ascontiguousarray(Wh[:, E:E + 1024]),
        "whgb": np.ascontiguousarray(Wh[:, E + 1024:]),
        "bhv": np.ascontiguousarray(bh[:E].reshape(1, E)),
        "bhg": np.ascontiguousarray(bh[E:].reshape(ECH, 128).T),
        "wqk": np.ascontiguousarray(Wqk),
        "bqk": np.ascontiguousarray(bqk.reshape(128, 1)),
        "qkw": np.ascontiguousarray(qk_w.T),
        "qkb": np.ascontiguousarray(qk_b.T),
        "lnw": np.ascontiguousarray(lnw.reshape(FCH, 128).T),
        "lnb": np.ascontiguousarray(lnb.reshape(FCH, 128).T),
        "rel": np.ascontiguousarray(rel.reshape(1, 32)),
        "wout": np.ascontiguousarray(Wout),
        "bout": np.ascontiguousarray(bout.reshape(1, D)),
        "masks": M, "neg": NEG, "ident": ident,
        "makerbd": makerbd, "upbd": upbd,
        "wct": np.ascontiguousarray(wc.T),
    }
    in_maps = []
    for c in range(NCORES):
        m = dict(common)
        m["x0t"] = np.ascontiguousarray(x[c * T:(c + 1) * T, :].T)
        pm = np.zeros((128, 8), dtype=np.float32)
        pm[:, :c] = 1.0
        m["pmask"] = pm
        in_maps.append(m)
    return in_maps


def chunkT(ap, nch, row_elems, ch0=0, nsub=None):
    if nsub is None:
        nsub = nch
    return bass.AP(ap.tensor, ch0 * 128 * row_elems,
                   [[row_elems, 128], [128 * row_elems, nsub], [1, row_elems]])


def build_nc(stage=STAGE, debug=DEBUG):
    nc = bass.Bass("TRN2", target_bir_lowering=False, debug=False,
                   detect_race_conditions=False)

    x0t = nc.dram_tensor("x0t", [D, T], F32, kind="ExternalInput").ap()
    whva_d = nc.dram_tensor("whva", [D, 1024], F32, kind="ExternalInput").ap()
    whvb_d = nc.dram_tensor("whvb", [D, 1024], F32, kind="ExternalInput").ap()
    whga_d = nc.dram_tensor("whga", [D, 1024], F32, kind="ExternalInput").ap()
    whgb_d = nc.dram_tensor("whgb", [D, 1024], F32, kind="ExternalInput").ap()
    bhv_d = nc.dram_tensor("bhv", [1, E], F32, kind="ExternalInput").ap()
    bhg_d = nc.dram_tensor("bhg", [128, ECH], F32, kind="ExternalInput").ap()
    wqk_d = nc.dram_tensor("wqk", [D, QKD], F32, kind="ExternalInput").ap()
    bqk_d = nc.dram_tensor("bqk", [128, 1], F32, kind="ExternalInput").ap()
    qkw_d = nc.dram_tensor("qkw", [128, 4], F32, kind="ExternalInput").ap()
    qkb_d = nc.dram_tensor("qkb", [128, 4], F32, kind="ExternalInput").ap()
    lnw_d = nc.dram_tensor("lnw", [128, FCH], F32, kind="ExternalInput").ap()
    lnb_d = nc.dram_tensor("lnb", [128, FCH], F32, kind="ExternalInput").ap()
    rel_d = nc.dram_tensor("rel", [1, 32], F32, kind="ExternalInput").ap()
    wout_d = nc.dram_tensor("wout", [E, D], F32, kind="ExternalInput").ap()
    bout_d = nc.dram_tensor("bout", [1, D], F32, kind="ExternalInput").ap()
    masks_d = nc.dram_tensor("masks", [32, 128, 512], F32, kind="ExternalInput").ap()
    neg_d = nc.dram_tensor("neg", [128, 512], F32, kind="ExternalInput").ap()
    ident_d = nc.dram_tensor("ident", [128, 128], F32, kind="ExternalInput").ap()
    makerbd_d = nc.dram_tensor("makerbd", [32, 128, 128], F32, kind="ExternalInput").ap()
    upbd_d = nc.dram_tensor("upbd", [32, 128, 512], F32, kind="ExternalInput").ap()
    wct_d = nc.dram_tensor("wct", [FF, D], F32, kind="ExternalInput").ap()
    pmask_d = nc.dram_tensor("pmask", [128, 8], F32, kind="ExternalInput").ap()

    out_d = nc.dram_tensor("out", [D, T], F32, kind="ExternalOutput").ap()
    spill = "ExternalOutput" if debug else "Internal"
    v_sp = nc.dram_tensor("v_sp", [NG, 128, 2 * E], BF16, kind=spill).ap()
    p_sp = nc.dram_tensor("p_sp", [NG, 128, E], BF16, kind=spill).ap()
    nsc_sp = nc.dram_tensor("nsc_sp", [128, FCH * T], BF16, kind=spill).ap()
    x1_sp = nc.dram_tensor("x1_sp", [NG, 128, OCH * G], F32, kind=spill).ap()
    x1b_sp = nc.dram_tensor("x1b_sp", [NG, 128, OCH * G], BF16, kind=spill).ap()
    cc_in = nc.dram_tensor("cc_in", [128, E], BF16)
    cc_out = nc.dram_tensor("cc_out", [NCORES * 128, E], BF16, addr_space="Shared")
    cc_bar_in = nc.dram_tensor("cc_bar_in", [1, 16], BF16)
    cc_bar_out = nc.dram_tensor("cc_bar_out", [NCORES, 16], BF16, addr_space="Shared")
    if debug:
        dbg1 = nc.dram_tensor("dbg1", [128, T], F32, kind="ExternalOutput").ap()
        dbg2 = nc.dram_tensor("dbg2", [128, 512], F32, kind="ExternalOutput").ap()
        dbg3 = nc.dram_tensor("dbg3", [8, T], F32, kind="ExternalOutput").ap()
        dbg4 = nc.dram_tensor("dbg4", [128, 4 * T], BF16, kind="ExternalOutput").ap()
        dbg5 = nc.dram_tensor("dbg5", [128, E], F32, kind="ExternalOutput").ap()

    es = contextlib.ExitStack()
    es012 = contextlib.ExitStack()
    es0 = contextlib.ExitStack()
    es01 = contextlib.ExitStack()

    def mk(stk, name, shape, dt=F32):
        return stk.enter_context(nc.sbuf_tensor(name, shape, dt))
    def sem(name):
        return es.enter_context(nc.semaphore(name))

    bhg_sb = mk(es, "bhg_sb", [128, ECH])
    bqk_sb = mk(es, "bqk_sb", [128, 1])
    qkw_sb = mk(es, "qkw_sb", [128, 4])
    qkb_sb = mk(es, "qkb_sb", [128, 4])
    qkws_sb = mk(es, "qkws_sb", [128, 2])
    lnw_sb = mk(es, "lnw_sb", [128, FCH])
    lnb_sb = mk(es, "lnb_sb", [128, FCH])
    rel_sb = mk(es, "rel_sb", [1, 32])
    rels_sb = mk(es, "rels_sb", [1, 32])
    ident_sb = mk(es, "ident_sb", [128, 128])
    identb_sb = mk(es, "identb_sb", [128, 128], BF16)
    pmask_sb = mk(es, "pmask_sb", [128, 8])
    ones1f = mk(es, "ones1f", [1, 128])
    ones1b = mk(es, "ones1b", [1, 128], BF16)
    onescol = mk(es, "onescol", [128, 1])
    onesmat = mk(es, "onesmat", [128, 128])
    onesmatb = mk(es, "onesmatb", [128, 128], BF16)
    onesrowb = mk(es, "onesrowb", [1, G], BF16)
    bhv_sb = mk(es, "bhv_sb", [1, E], BF16)
    bout_sb = mk(es, "bout_sb", [1, D], BF16)

    qq_sb = mk(es012, "qq_sb", [128, T], BF16)
    qkk_sb = mk(es012, "qkk_sb", [128, T], BF16)
    lq_sb = mk(es012, "lq_sb", [128, T], BF16)
    biasT_sb = mk(es012, "biasT_sb", [128, 512])
    tcol_sb = mk(es012, "tcol_sb", [128, 32])
    bcA1_sb = mk(es012, "bcA1_sb", [128, T])
    S_sb = mk(es012, "S_sb", [128, E], BF16)
    negb_sb = mk(es012, "negb_sb", [1, T], BF16)
    whg0_sb = mk(es012, "whg0_sb", [128, FCH * 1024], BF16)

    nsc_sb = mk(es01, "nsc_sb", [128, FCH * T], BF16)
    lkt_sb = mk(es01, "lkt_sb", [128, 16 * 128], BF16)
    wqks_sb = mk(es01, "wqks_sb", [128, FCH * QKD], BF16)
    whv_sb = mk(es01, "whv_sb", [128, FCH * E], BF16)

    x0b_sb = [mk(es0, f"x0b_sb{i}", [128, T], BF16) for i in range(2)]
    xsqb_sb = mk(es0, "xsqb_sb", [128, T], BF16)
    x0f_sb = [mk(es0, f"x0f_sb{i}", [128, T]) for i in range(2)]
    xsq_sb = mk(es0, "xsq_sb", [128, T])
    lkf_sb = mk(es0, "lkf_sb", [128, T], BF16)
    r0 = mk(es0, "r0", [128, T]); r1 = mk(es0, "r1", [128, T])
    r2 = mk(es0, "r2", [128, T]); r3 = mk(es0, "r3", [128, T])

    d_par = sem("d_par"); d_parg = sem("d_parg"); d_x0 = sem("d_x0")
    d_msk = [sem("d_msk0"), sem("d_msk1")]
    d_wqk = sem("d_wqk")
    d_whv = [sem("d_whva"), sem("d_whvb")]
    d_whg = [sem("d_whga"), sem("d_whgb")]
    d_wout = sem("d_wout")
    d_x0b = [sem("d_x0b0"), sem("d_x0b1")]
    d_x0f = [sem("d_x0f0"), sem("d_x0f1")]
    d_x1b = [sem("d_x1b0"), sem("d_x1b1")]
    d_vw = [sem("d_vw0"), sem("d_vw1")]
    d_pw = [sem("d_pw0"), sem("d_pw1")]
    d_nw = sem("d_nw")
    d_vr = [sem("d_vr0"), sem("d_vr1")]
    d_pr = [sem("d_pr0"), sem("d_pr1")]
    d_nr = [sem("d_nr0"), sem("d_nr1")]
    d_x0g = [sem("d_x0g0"), sem("d_x0g1")]
    d_x1w = [sem("d_x1w0"), sem("d_x1w1")]
    d_ci = sem("d_ci")
    d_co = [sem("d_co0"), sem("d_co1")]
    d_mk = sem("d_mk")
    d_up = [sem("d_up0"), sem("d_up1")]
    d_w3 = [sem("d_w30"), sem("d_w31")]
    d_x1r = [sem("d_x1r0"), sem("d_x1r1")]
    d_x1h = [sem("d_x1h0"), sem("d_x1h1")]
    d_x1h4 = [d_x1h[0], d_x1h[1], d_x1r[0], d_x1r[1]]
    d_out = sem("d_out"); d_dbg = sem("d_dbg")
    v_ones = sem("v_ones"); s_sq = sem("s_sq"); t_st = sem("t_st")
    v_r1 = sem("v_r1"); s_r1 = sem("s_r1"); v_r2 = sem("v_r2"); s_r2 = sem("s_r2")
    v_rows = sem("v_rows"); t_bc = sem("t_bc"); v_bc = sem("v_bc")
    v_nsc = sem("v_nsc"); t_qk = sem("t_qk"); s_qk = sem("s_qk")
    v_quads = sem("v_quads"); t_lkT = sem("t_lkT"); v_lkT = sem("v_lkT")
    s_rel = sem("s_rel"); t_tcol = sem("t_tcol"); v_tcol = sem("v_tcol")
    v_msk = sem("v_msk"); v_bias = sem("v_bias")
    t_v = sem("t_v"); s_vsil = sem("s_vsil"); t_lin = sem("t_lin"); v_lkv = sem("v_lkv")
    s_snap = sem("s_snap"); s_L = sem("s_L"); cc_done = sem("cc_done")
    v_Ss = sem("v_Ss")
    t_sim = sem("t_sim"); v_sim1 = sem("v_sim1"); s_relu = sem("s_relu")
    v_atn = sem("v_atn"); t_gate = sem("t_gate"); s_gate = sem("s_gate")
    t_B = sem("t_B"); v_hid = sem("v_hid"); t_Bg = sem("t_Bg")
    t_wout = sem("t_wout"); v_x1 = sem("v_x1")
    s_sq3 = sem("s_sq3"); t_st3 = sem("t_st3")
    v_r31 = sem("v_r31"); s_r31 = sem("s_r31"); v_rows3 = sem("v_rows3")
    t_bc3 = sem("t_bc3"); v_bc3 = sem("v_bc3"); v_n3 = sem("v_n3")
    t_D = sem("t_D"); v_D = sem("v_D")
    t_z = sem("t_z"); s_z = sem("s_z"); t_cp = sem("t_cp"); v_acc = sem("v_acc")
    v_out = sem("v_out")

    NPAR = 9

    ps_st = contextlib.ExitStack()
    st1_ps = ps_st.enter_context(nc.psum_tensor("st1_ps", [128, T], F32))
    st2_ps = ps_st.enter_context(nc.psum_tensor("st2_ps", [128, T], F32))

    with nc.Block() as blk:
        @blk.sync
        def _(sy):
            for dst, src in [(bhg_sb, bhg_d), (bqk_sb, bqk_d), (qkw_sb, qkw_d),
                             (qkb_sb, qkb_d), (lnw_sb, lnw_d), (lnb_sb, lnb_d),
                             (rel_sb, rel_d), (ident_sb, ident_d), (pmask_sb, pmask_d)]:
                sy.dma_start(out=dst[:], in_=src[:, :]).then_inc(d_par, 16)

        @blk.gpsimd
        def _(g_):
            for ch in range(FCH):
                if ch >= 2:
                    g_.wait_ge(t_st, ch - 1)
                g_.dma_start(out=x0b_sb[ch % 2][:, :],
                             in_=x0t[ch * 128:(ch + 1) * 128, :]
                             ).then_inc(d_x0b[ch % 2], 16)
            g_.dma_start(out=wqks_sb[:, :],
                         in_=chunkT(wqk_d, FCH, QKD)).then_inc(d_wqk, 16)
            g_.dma_start(out=bhv_sb[:, :], in_=bhv_d[:, :]).then_inc(d_parg, 16)
            g_.dma_start(out=bout_sb[:, :], in_=bout_d[:, :]).then_inc(d_parg, 16)
            g_.dma_start(out=identb_sb[:, :], in_=ident_d[:, :]).then_inc(d_parg, 16)
            g_.dma_start(out=cc_bar_in[0:1, :], in_=x0t[0:1, 0:16]).then_inc(d_parg, 16)
            for half, src in ((0, whva_d), (1, whvb_d)):
                g_.dma_start(
                    out=bass.AP(whv_sb.ap().tensor, half * 1024,
                                [[FCH * E, 128], [E, FCH], [1, 1024]]),
                    in_=chunkT(src, FCH, 1024),
                ).then_inc(d_whv[half], 16)
            g_.wait_ge(d_parg, 64)
            g_.collective_compute(
                "AllGather", ALU.bypass,
                replica_groups=[list(range(NCORES))],
                ins=[cc_bar_in.ap().opt()],
                outs=[cc_bar_out.ap().opt()],
            ).then_inc(cc_done, 1)
            g_.wait_ge(cc_done, 1)

        @blk.vector
        def _(v):
            v.memset(ones1f[:], 1.0)
            v.memset(ones1b[:], 1.0)
            v.memset(onescol[:], 1.0)
            v.memset(onesmat[:], 1.0)
            v.memset(onesmatb[:], 1.0)
            v.memset(onesrowb[:], 1.0).then_inc(v_ones, 1)
            v.wait_ge(d_par, 16 * NPAR)
            v.wait_ge(d_parg, 48)
            v.tensor_scalar(qkws_sb[:, 0:1], qkw_sb[:, 0:1], 1.0 / G, None, ALU.mult)
            v.tensor_scalar(qkws_sb[:, 1:2], qkb_sb[:, 0:1], 1.0 / G, None, ALU.mult).then_inc(v_ones, 1)

        @blk.scalar
        def _(s):
            for ch in range(FCH):
                s.wait_ge(d_x0b[ch % 2], 16 * (ch // 2 + 1))
                if ch >= 1:
                    s.wait_ge(t_st, ch)
                s.activation(xsqb_sb[:, :], x0b_sb[ch % 2][:, :],
                             AF.Square).then_inc(s_sq, 1)
            s.wait_ge(d_par, 16 * NPAR)
            s.wait_ge(d_parg, 48)
            s.activation(rels_sb[:, :], rel_sb[:, :], AF.Copy, scale=BIAS_SCALE).then_inc(s_rel, 1)
            s.wait_ge(v_r1, 1)
            s.activation(r2[:, :], r3[:, :], AF.Sqrt).then_inc(s_r1, 1)
            s.wait_ge(v_r2, 1)
            s.activation(r1[:, :], r3[:, :], AF.Sqrt).then_inc(s_r2, 1)

        @blk.tensor
        def _(t):
            t.wait_ge(v_ones, 1)
            for ch in range(FCH):
                t.wait_ge(d_x0b[ch % 2], 16 * (ch // 2 + 1))
                for sl in range(4):
                    t.matmul(st1_ps[:, sl * 512:(sl + 1) * 512], onesmatb[:],
                             x0b_sb[ch % 2][:, sl * 512:(sl + 1) * 512],
                             start=(ch == 0), stop=(ch == FCH - 1))
                t.wait_ge(s_sq, ch + 1)
                for sl in range(4):
                    ins = t.matmul(st2_ps[:, sl * 512:(sl + 1) * 512], onesmatb[:],
                                   xsqb_sb[:, sl * 512:(sl + 1) * 512],
                                   start=(ch == 0), stop=(ch == FCH - 1))
                ins.then_inc(t_st, 1)

        @blk.vector
        def _(v):
            v.wait_ge(t_st, FCH)
            v.tensor_scalar(r0[:, :], st1_ps[:, :], 1.0 / D, None, ALU.mult)
            v.tensor_scalar(r1[:, :], st2_ps[:, :], 1.0 / D, None, ALU.mult)
            v.tensor_mul(r2[:, :], r0[:, :], r0[:, :])
            v.tensor_sub(r1[:, :], r1[:, :], r2[:, :])
            v.tensor_scalar(r2[:, :], r1[:, :], EPS, None, ALU.add)
            v.reciprocal(r3[:, :], r2[:, :]).then_inc(v_r1, 1)
            v.wait_ge(s_r1, 1)
            v.tensor_mul(r0[:, :], r0[:, :], r2[:, :])
            v.tensor_mul(r1[:, :], r1[:, :], r3[:, :])
            v.tensor_scalar(r1[:, :], r1[:, :], EPS, None, ALU.add)
            v.reciprocal(r3[:, :], r1[:, :]).then_inc(v_r2, 1)
            v.wait_ge(s_r2, 1)
            v.tensor_scalar(negb_sb[:, :], r0[0:1, :], -1.0, None, ALU.mult)
            v.tensor_mul(r3[:, :], r2[:, :], r1[:, :])
            v.tensor_mul(r0[:, :], r0[:, :], r1[:, :])
            v.tensor_scalar(r2[:, :], r2[:, :], 1.0, None, ALU.add).then_inc(v_rows, 1)

    if TRUNC == 1:
        with nc.Block() as blk:
            @blk.sync
            def _(sy):
                sy.wait_ge(t_st, FCH)
                sy.wait_ge(v_rows, 1)
                sy.wait_ge(s_rel, 1)
        ps_st.close(); es0.close(); es01.close(); es012.close(); es.close()
        return nc

    ps_st.close()
    ps_qk = contextlib.ExitStack()
    qk_ps = ps_qk.enter_context(nc.psum_tensor("qk_ps", [128, T], F32))

    with nc.Block() as blk:
        @blk.sync
        def _(sy):
            for ch in range(FCH):
                if ch >= 2:
                    sy.wait_ge(v_nsc, ch - 1)
                sy.dma_start(out=x0f_sb[ch % 2][:, :],
                             in_=x0t[ch * 128:(ch + 1) * 128, :]
                             ).then_inc(d_x0f[ch % 2], 16)

        @blk.vector
        def _(v):
            for ch in range(FCH):
                v.wait_ge(d_x0f[ch % 2], 16 * (ch // 2 + 1))
                v.tensor_mul(xsq_sb[:, :], x0f_sb[ch % 2][:, :], r3[:, :])
                v.tensor_sub(xsq_sb[:, :], xsq_sb[:, :], r0[:, :])
                v.tensor_scalar(nsc_sb[:, ch * T:(ch + 1) * T], xsq_sb[:, :],
                                lnw_sb[:, ch:ch + 1], lnb_sb[:, ch:ch + 1],
                                ALU.mult, ALU.add).then_inc(v_nsc, 1)
            v.tensor_copy(bcA1_sb[:, :], r2[:, :]).then_inc(v_bc, 1)

        @blk.tensor
        def _(t):
            t.wait_ge(v_r1, 1)
            t.wait_ge(v_rows, 1)
            t.wait_ge(d_wqk, 16)
            for ch in range(FCH):
                t.wait_ge(v_nsc, ch + 1)
                for sl in range(4):
                    t.matmul(qk_ps[:, sl * 512:(sl + 1) * 512],
                             wqks_sb[:, ch * QKD:(ch + 1) * QKD],
                             nsc_sb[:, ch * T + sl * 512: ch * T + (sl + 1) * 512],
                             start=(ch == 0), stop=(ch == FCH - 1))
            t.nop().then_inc(t_qk, 1)

        @blk.scalar
        def _(s):
            s.wait_ge(t_qk, 1)
            s.wait_ge(v_nsc, FCH)
            s.activation(xsq_sb[:, :], qk_ps[:, :], AF.Silu, bias=bqk_sb[:, 0:1]).then_inc(s_qk, 1)

        @blk.vector
        def _(v):
            v.wait_ge(s_qk, 1)
            v.tensor_scalar(qq_sb[:, :], xsq_sb[:, :], qkws_sb[:, 0:1], qkws_sb[:, 1:2],
                            ALU.mult, ALU.add).then_inc(v_quads, 1)
            v.tensor_scalar(lq_sb[:, :], xsq_sb[:, :], qkw_sb[:, 1:2], qkb_sb[:, 1:2],
                            ALU.mult, ALU.add).then_inc(v_quads, 1)
            v.tensor_scalar(qkk_sb[:, :], xsq_sb[:, :], qkw_sb[:, 2:3], qkb_sb[:, 2:3],
                            ALU.mult, ALU.add).then_inc(v_quads, 1)
            v.tensor_scalar(lkf_sb[:, :], xsq_sb[:, :], qkw_sb[:, 3:4], qkb_sb[:, 3:4],
                            ALU.mult, ALU.add).then_inc(v_quads, 1)

    if TRUNC == 2:
        with nc.Block() as blk:
            @blk.sync
            def _(sy):
                sy.wait_ge(v_nsc, FCH)
                sy.wait_ge(t_qk, 1)
                sy.wait_ge(s_qk, 1)
                sy.wait_ge(v_quads, 4)
                sy.wait_ge(v_bc, 1)
        ps_qk.close(); es0.close(); es01.close(); es012.close(); es.close()
        return nc

    ps_lkt = contextlib.ExitStack()
    lkt_ps = [ps_lkt.enter_context(nc.psum_tensor(f"lkt_ps{i}", [128, 128], BF16))
              for i in range(2)]
    tcol_ps = ps_lkt.enter_context(nc.psum_tensor("tcol_ps", [128, 32], F32))

    with nc.Block() as blk:
        @blk.tensor
        def _(t):
            t.wait_ge(s_rel, 1)
            t.matmul(tcol_ps[:, :], ones1f[:], rels_sb[0:1, :],
                     start=True, stop=True).then_inc(t_tcol, 1)
            t.wait_ge(v_quads, 4)
            for tc in range(16):
                if tc >= 2:
                    t.wait_ge(v_lkT, tc - 1)
                t.transpose(lkt_ps[tc % 2][:, :], lkf_sb[:, tc * 128:(tc + 1) * 128],
                            identb_sb[:]).then_inc(t_lkT, 1)

        @blk.vector
        def _(v):
            v.wait_ge(t_tcol, 1)
            v.tensor_copy(tcol_sb[:, :], tcol_ps[:, :]).then_inc(v_tcol, 1)
            for tc in range(16):
                v.wait_ge(t_lkT, tc + 1)
                v.tensor_copy(lkt_sb[:, tc * 128:(tc + 1) * 128], lkt_ps[tc % 2][:, :]).then_inc(v_lkT, 1)

    ps_lkt.close()

    if TRUNC == 3:
        with nc.Block() as blk:
            @blk.sync
            def _(sy):
                sy.wait_ge(v_tcol, 1)
                sy.wait_ge(v_lkT, 16)
        ps_qk.close(); es0.close(); es01.close(); es012.close(); es.close()
        return nc

    if stage == 0:
        with nc.Block() as blk:
            @blk.sync
            def _(sy):
                n = 0
                if debug:
                    sy.wait_ge(v_nsc, FCH)
                    sy.dma_start(out=nsc_sp[:, :], in_=nsc_sb[:, :]).then_inc(d_dbg, 16); n += 1
                    sy.wait_ge(s_qk, 1)
                    sy.dma_start(out=dbg1[:, :], in_=xsq_sb[:, :]).then_inc(d_dbg, 16); n += 1
                    sy.wait_ge(v_rows, 1)
                    n += 1
                    for r_ in (r0, r1, r2, r3):
                        sy.dma_start(out=dbg3[n - 3:n - 2, :], in_=r_[0:1, :]).then_inc(d_dbg, 16); n += 1
                    sy.wait_ge(v_quads, 4)
                    sy.wait_ge(v_lkT, 16)
                    sy.dma_start(out=dbg4[:, 0:T], in_=qq_sb[:, :]).then_inc(d_dbg, 16); n += 1
                    sy.dma_start(out=dbg4[:, T:2 * T], in_=qkk_sb[:, :]).then_inc(d_dbg, 16); n += 1
                    sy.dma_start(out=dbg4[:, 2 * T:3 * T], in_=lq_sb[:, :]).then_inc(d_dbg, 16); n += 1
                    sy.dma_start(out=dbg4[:, 3 * T:4 * T], in_=lkt_sb[:, :]).then_inc(d_dbg, 16); n += 1
                    sy.dma_start(out=dbg5[:, 0:T], in_=bcA1_sb[:, :]).then_inc(d_dbg, 16); n += 1
                    sy.wait_ge(d_dbg, 16 * (n - 1))
                else:
                    sy.wait_ge(v_quads, 4)
                    sy.wait_ge(v_lkT, 16)
        ps_qk.close(); es0.close(); es01.close(); es012.close(); es.close()
        return nc

    es0.close()
    es1 = contextlib.ExitStack()
    vz_sb = [mk(es1, f"vz_sb{i}", [128, 2 * E], BF16) for i in range(2)]
    pz_sb = [mk(es1, f"pz_sb{i}", [128, E], BF16) for i in range(2)]
    L_sb = mk(es1, "L_sb", [128, E], BF16)
    prun_sb = mk(es1, "prun_sb", [128, E])
    mtile_sb = mk(es1, "mtile_sb", [128, 2 * 512])


    ps_qk.close()
    ps1 = contextlib.ExitStack()
    pv = [ps1.enter_context(nc.psum_tensor(f"pv{i}", [128, 1024], F32)) for i in range(2)]
    linkv_ps = ps1.enter_context(nc.psum_tensor("linkv_ps", [128, E], F32))

    with nc.Block() as blk:
        @blk.gpsimd
        def _(g_):
            g_.dma_start(
                out=bass.AP(whg0_sb.ap().tensor, 0,
                            [[FCH * 1024, 128], [1024, FCH], [1, 1024]]),
                in_=chunkT(whga_d, FCH, 1024),
            ).then_inc(d_whg[0], 16)

        @blk.sync
        def _(sy):
            if TRUNC == 0 or TRUNC >= 4:
                sy.dma_start(out=mtile_sb[:, 0:512], in_=neg_d[:, :]).then_inc(d_msk[0], 16)
                for b in range(32):
                    j = b + 1
                    if j >= 2:
                        sy.wait_ge(v_msk, j - 1)
                    sy.dma_start(out=mtile_sb[:, (j % 2) * 512:(j % 2 + 1) * 512],
                                 in_=masks_d[b, :, :]).then_inc(d_msk[j % 2], 16)

        @blk.vector
        def _(v):
            if TRUNC == 0 or TRUNC >= 4:
                v.wait_ge(d_msk[0], 16)
                v.tensor_copy(biasT_sb[:, :], mtile_sb[:, 0:512]).then_inc(v_msk, 1)
                for b in range(32):
                    j = b + 1
                    v.wait_ge(d_msk[j % 2], 16 * (j // 2 + 1))
                    v.scalar_tensor_tensor(biasT_sb[:, :],
                                           mtile_sb[:, (j % 2) * 512:(j % 2 + 1) * 512],
                                           tcol_sb[:, b:b + 1], biasT_sb[:, :],
                                           ALU.mult, ALU.add).then_inc(v_msk, 1)

        @blk.vector
        def _(v):
            v.memset(pz_sb[0][:, :], 0.0).then_inc(s_snap, 1)

        @blk.tensor
        def _(t):
            t.wait_ge(s_qk, 1)
            t.wait_ge(v_lkT, 16)
            t.wait_ge(v_tcol, 1)
            for g in range(NG):
                for r in range(4):
                    tc, eh = divmod(r, 2)
                    if g == 0 and r <= 1:
                        t.wait_ge(d_whv[eh], 16)
                    rr = 4 * g + r
                    if rr >= 2:
                        t.wait_ge(s_vsil, rr - 1)
                    pv_ = pv[r % 2]
                    for ch in range(FCH):
                        for sl in range(2):
                            t.matmul(
                                pv_[:, sl * 512:(sl + 1) * 512],
                                nsc_sb[:, ch * T + g * G + tc * 128:
                                       ch * T + g * G + tc * 128 + 128],
                                whv_sb[:, ch * E + eh * 1024 + sl * 512:
                                       ch * E + eh * 1024 + (sl + 1) * 512],
                                start=(ch == 0), stop=False)
                    for sl in range(2):
                        ins = t.matmul(pv_[:, sl * 512:(sl + 1) * 512], ones1b[:],
                                       bhv_sb[0:1, eh * 1024 + sl * 512:
                                              eh * 1024 + (sl + 1) * 512],
                                       start=False, stop=True)
                    ins.then_inc(t_v, 1)
                t.wait_ge(s_vsil, 4 * (g + 1))
                if g >= 1:
                    t.wait_ge(v_lkv, g)
                for jc in range(2):
                    for ns in range(4):
                        ins = t.matmul(
                            linkv_ps[:, ns * 512:(ns + 1) * 512],
                            lkt_sb[:, (2 * g + jc) * 128:(2 * g + jc) * 128 + 128],
                            vz_sb[g % 2][:, jc * E + ns * 512: jc * E + (ns + 1) * 512],
                            start=(jc == 0), stop=(jc == 1))
                ins.then_inc(t_lin, 1)

        @blk.scalar
        def _(s):
            for g in range(NG):
                for r in range(4):
                    tc, eh = divmod(r, 2)
                    s.wait_ge(t_v, 4 * g + r + 1)
                    if g >= 2 and r == 0:
                        s.wait_ge(d_vw[g % 2], 16 * (g // 2))
                    s.activation(vz_sb[g % 2][:, tc * E + eh * 1024:
                                              tc * E + (eh + 1) * 1024],
                                 pv[r % 2][:, :], AF.Silu).then_inc(s_vsil, 1)

        @blk.vector
        def _(v):
            for g in range(NG):
                v.wait_ge(t_lin, g + 1)
                if g == 0:
                    v.tensor_copy(prun_sb[:, :], linkv_ps[:, :]).then_inc(v_lkv, 1)
                else:
                    v.tensor_add(prun_sb[:, :], prun_sb[:, :],
                                 linkv_ps[:, :]).then_inc(v_lkv, 1)
                if g < NG - 1:
                    if g + 1 >= 2:
                        v.wait_ge(d_pw[(g + 1) % 2], 16 * ((g + 1) // 2))
                    v.tensor_scalar(pz_sb[(g + 1) % 2][:, :], prun_sb[:, :], 1.0 / G,
                                    None, ALU.mult).then_inc(s_snap, 1)
            v.tensor_scalar(L_sb[:, :], prun_sb[:, :], 1.0 / G,
                            None, ALU.mult).then_inc(s_L, 1)

        @blk.sync
        def _(sy):
            for g in range(NG):
                sy.wait_ge(s_vsil, 4 * (g + 1))
                sy.dma_start(out=v_sp[g, :, :],
                             in_=vz_sb[g % 2][:, :]).then_inc(d_vw[g % 2], 16)
                sy.wait_ge(s_snap, g + 1)
                sy.dma_start(out=p_sp[g, :, :],
                             in_=pz_sb[g % 2][:, :]).then_inc(d_pw[g % 2], 16)
            sy.wait_ge(t_v, 4 * NG)
            sy.dma_start(out=nsc_sp[:, :], in_=nsc_sb[:, :]).then_inc(d_nw, 16)


    if stage == 1:
        gathd = [mk(es1, f"gathd{i}", [128, E], BF16) for i in range(2)]
        Sd = mk(es1, "Sd", [128, E])
        with nc.Block() as blk:
            @blk.gpsimd
            def _(g_):
                g_.wait_ge(s_L, 1)
                g_.dma_start(out=cc_in[:, :], in_=L_sb[:, :]).then_inc(d_ci, 16)
                g_.wait_ge(d_ci, 16)
                g_.collective_compute(
                    "AllGather", ALU.bypass,
                    replica_groups=[list(range(NCORES))],
                    ins=[cc_in.ap().opt()],
                    outs=[cc_out.ap().opt()],
                ).then_inc(cc_done, 1)
                g_.wait_ge(cc_done, 2)
                for i in range(NCORES):
                    if i >= 2:
                        g_.wait_ge(v_Ss, i - 1)
                    g_.dma_start(out=gathd[i % 2][:, :],
                                 in_=cc_out[i * 128:(i + 1) * 128, :]
                                 ).then_inc(d_co[i % 2], 16)

            @blk.vector
            def _(v):
                v.memset(Sd[:, :], 0.0)
                for i in range(NCORES):
                    v.wait_ge(d_co[i % 2], 16 * (i // 2 + 1))
                    v.scalar_tensor_tensor(Sd[:, :], gathd[i % 2][:, :],
                                           pmask_sb[:, i:i + 1], Sd[:, :],
                                           ALU.mult, ALU.add).then_inc(v_Ss, 1)

            @blk.sync
            def _(sy):
                if debug:
                    sy.wait_ge(v_Ss, NCORES)
                    sy.dma_start(out=dbg5[:, :], in_=Sd[:, :]).then_inc(d_dbg, 16)
                    sy.wait_ge(d_dbg, 16)
                for dd in (d_vw, d_pw):
                    sy.wait_ge(dd[0], 16 * NG // 2)
                    sy.wait_ge(dd[1], 16 * NG // 2)
                sy.wait_ge(d_nw, 16)
                sy.wait_ge(v_Ss, NCORES)
        es1.close(); es01.close(); ps1.close(); es012.close(); es.close()
        return nc

    es1.close(); es01.close()
    es2 = contextlib.ExitStack()
    whg1_sb = mk(es2, "whg1_sb", [128, FCH * 1024], BF16)
    wout_sb = mk(es2, "wout_sb", [128, ECH * D], BF16)
    S_f32 = mk(es2, "S_f32", [128, E])
    gath_sb = [mk(es2, f"gath_sb{i}", [128, E], BF16) for i in range(2)]
    v2_sb = [mk(es2, f"v2_sb{i}", [128, 2 * E], BF16) for i in range(2)]
    p2_sb = [mk(es2, f"p2_sb{i}", [128, E], BF16) for i in range(2)]
    n2_sb = [mk(es2, f"n2_sb{i}", [128, FCH * G], BF16) for i in range(2)]
    x2_sb = [mk(es2, f"x2_sb{i}", [128, OCH * G]) for i in range(2)]
    hid_sb = [mk(es2, f"hid_sb{i}", [128, ECH * G], BF16) for i in range(2)]
    sims_sb = [mk(es2, f"sims_sb{i}", [128, 512]) for i in range(2)]
    relu_sb = mk(es2, "relu_sb", [128, 512])
    atn_sb = [mk(es2, f"atn_sb{i}", [128, 512], BF16) for i in range(NG)]

    ps1.close()
    ps2 = contextlib.ExitStack()
    sim_ps = ps2.enter_context(nc.psum_tensor("sim_ps", [128, 512], F32))
    gate_ps = [ps2.enter_context(nc.psum_tensor(f"gate_ps{i}", [128, G], F32))
               for i in range(2)]
    b_ps = [ps2.enter_context(nc.psum_tensor(f"b_ps{i}", [128, G], F32)) for i in range(2)]
    wo_ps = [ps2.enter_context(nc.psum_tensor(f"wo_ps{i}", [128, 512], F32))
             for i in range(2)]

    P1GATES = [(d_nw, 16), (d_vw[0], 16 * NG // 2), (d_vw[1], 16 * NG // 2),
               (d_pw[0], 16 * NG // 2), (d_pw[1], 16 * NG // 2),
               (d_ci, 16)]

    with nc.Block() as blk:
        @blk.gpsimd
        def _(g_):
            g_.wait_ge(s_L, 1)
            g_.dma_start(out=cc_in[:, :], in_=L_sb[:, :]).then_inc(d_ci, 16)
            g_.wait_ge(d_ci, 16)
            g_.collective_compute(
                "AllGather", ALU.bypass,
                replica_groups=[list(range(NCORES))],
                ins=[cc_in.ap().opt()],
                outs=[cc_out.ap().opt()],
            ).then_inc(cc_done, 1)
            for s_, v_ in P1GATES:
                g_.wait_ge(s_, v_)
            g_.dma_start(
                out=bass.AP(whg1_sb.ap().tensor, 0,
                            [[FCH * 1024, 128], [1024, FCH], [1, 1024]]),
                in_=chunkT(whgb_d, FCH, 1024),
            ).then_inc(d_whg[1], 16)
            g_.wait_ge(cc_done, 2)
            for i in range(NCORES):
                if i >= 2:
                    g_.wait_ge(v_Ss, i - 1)
                g_.dma_start(out=gath_sb[i % 2][:, :],
                             in_=cc_out[i * 128:(i + 1) * 128, :]).then_inc(d_co[i % 2], 16)
            for i in range(4):
                g_.dma_start(
                    out=wout_sb[:, i * 4 * D:(i + 1) * 4 * D],
                    in_=chunkT(wout_d, ECH, D, ch0=4 * i, nsub=4),
                ).then_inc(d_wout, 16)
            for g in range(NG):
                g_.wait_ge(d_x1w[g % 2], 16 * (g // 2 + 1))
                g_.dma_start(out=x1b_sp[g, :, :],
                             in_=x1_sp[g, :, :]).then_inc(d_up[1], 16)

        @blk.sync
        def _(sy):
            for s_, v_ in P1GATES:
                sy.wait_ge(s_, v_)
            for g in range(NG):
                if g >= 2:
                    sy.wait_ge(t_Bg, g - 1)
                    sy.wait_ge(v_x1, 4 * (g - 1))
                sy.wait_ge(d_vw[g % 2], 16 * (g // 2 + 1))
                sy.wait_ge(d_pw[g % 2], 16 * (g // 2 + 1))
                sy.dma_start(out=v2_sb[g % 2][:, :],
                             in_=v_sp[g, :, :]).then_inc(d_vr[g % 2], 16)
                sy.dma_start(out=p2_sb[g % 2][:, :],
                             in_=p_sp[g, :, :]).then_inc(d_pr[g % 2], 16)
                sy.dma_start(
                    out=n2_sb[g % 2][:, :],
                    in_=bass.AP(nsc_sp.tensor, g * G,
                                [[FCH * T, 128], [T, FCH], [1, G]]),
                ).then_inc(d_nr[g % 2], 16)
                if g >= 2:
                    sy.wait_ge(d_x1w[g % 2], 16 * (g // 2))
                sy.dma_start(
                    out=x2_sb[g % 2][:, :],
                    in_=bass.AP(x0t.tensor, g * G,
                                [[T, 128], [128 * T, OCH], [1, G]]),
                ).then_inc(d_x0g[g % 2], 16)
                if g >= 1:
                    sy.wait_ge(v_x1, 4 * g)
                    sy.dma_start(out=x1_sp[g - 1, :, :],
                                 in_=x2_sb[(g - 1) % 2][:, :]
                                 ).then_inc(d_x1w[(g - 1) % 2], 16)
            sy.wait_ge(v_x1, 4 * NG)
            sy.dma_start(out=x1_sp[NG - 1, :, :],
                         in_=x2_sb[(NG - 1) % 2][:, :]
                         ).then_inc(d_x1w[(NG - 1) % 2], 16)

        @blk.tensor
        def _(t):
            for g in range(NG):
                t.wait_ge(v_sim1, g)
                for jc in range(2):
                    ins = t.matmul(sim_ps[:, jc * 256:(jc + 1) * 256],
                                   qkk_sb[:, g * G + jc * 128: g * G + jc * 128 + 128],
                                   qq_sb[:, g * G:(g + 1) * G], start=True, stop=True)
                ins.then_inc(t_sim, 1)
            def gates(t, g, m):
                idx = ECH * g + m
                if g == 0 and m in (0, 8):
                    t.wait_ge(d_whg[m // 8], 16)
                if idx >= 2:
                    t.wait_ge(s_gate, idx - 1)
                gp = gate_ps[m % 2]
                whgh = whg0_sb if m < 8 else whg1_sb
                for ch in range(FCH):
                    ins = t.matmul(gp[:, :],
                                   whgh[:, ch * 1024 + (m % 8) * 128:
                                        ch * 1024 + (m % 8) * 128 + 128],
                                   n2_sb[g % 2][:, ch * G:(ch + 1) * G],
                                   start=(ch == 0), stop=(ch == FCH - 1))
                ins.then_inc(t_gate, 1)
            for g in range(2):
                t.wait_ge(d_nr[g % 2], 16)
                for m in range(ECH):
                    gates(t, g, m)
            for g in range(NG):
                t.wait_ge(d_vr[g % 2], 16 * (g // 2 + 1))
                t.wait_ge(d_pr[g % 2], 16 * (g // 2 + 1))
                if g >= 2:
                    t.wait_ge(d_nr[g % 2], 16 * (g // 2 + 1))
                if g == 0:
                    t.wait_ge(v_Ss, NCORES + 1)
                for m in range(ECH):
                    idx = ECH * g + m
                    if g >= 2:
                        gates(t, g, m)
                    if idx >= 2:
                        t.wait_ge(v_hid, idx - 1)
                    if m == 0:
                        t.wait_ge(v_atn, g + 1)
                    bp = b_ps[m % 2]
                    for jc in range(2):
                        t.matmul(bp[:, :],
                                 v2_sb[g % 2][:, jc * E + m * 128: jc * E + m * 128 + 128],
                                 atn_sb[g][:, jc * 256:(jc + 1) * 256],
                                 start=(jc == 0), stop=False)
                    t.matmul(bp[:, :], p2_sb[g % 2][:, m * 128:(m + 1) * 128],
                             lq_sb[:, g * G:(g + 1) * G], start=False, stop=False)
                    t.matmul(bp[:, :], S_sb[:, m * 128:(m + 1) * 128],
                             lq_sb[:, g * G:(g + 1) * G],
                             start=False, stop=True).then_inc(t_B, 1)
                t.nop().then_inc(t_Bg, 1)
                t.wait_ge(v_hid, ECH * (g + 1))
                t.wait_ge(d_wout, 64)
                for p_ in range(4):
                    pi = 4 * g + p_
                    if pi >= 2:
                        t.wait_ge(v_x1, pi - 1)
                    wp = wo_ps[p_ % 2]
                    for half in range(2):
                        o = 2 * p_ + half
                        for ek in range(ECH):
                            t.matmul(wp[:, half * 256:(half + 1) * 256],
                                     wout_sb[:, ek * D + o * 128: ek * D + o * 128 + 128],
                                     hid_sb[g % 2][:, ek * G:(ek + 1) * G],
                                     start=(ek == 0), stop=False)
                        t.matmul(wp[:, half * 256:(half + 1) * 256],
                                 bout_sb[0:1, o * 128:(o + 1) * 128],
                                 onesrowb[0:1, :], start=False, stop=False)
                        ins = t.matmul(wp[:, half * 256:(half + 1) * 256], ones1b[:],
                                       negb_sb[0:1, g * G:(g + 1) * G],
                                       start=False, stop=True)
                    ins.then_inc(t_wout, 1)

        @blk.scalar
        def _(s):
            for g in range(NG):
                s.wait_ge(v_sim1, g + 1)
                s.wait_ge(v_atn, g)
                s.activation(relu_sb[:, :], sims_sb[g % 2][:, :], AF.Relu).then_inc(s_relu, 1)
            for g in range(NG):
                for m in range(ECH):
                    idx = ECH * g + m
                    s.wait_ge(t_gate, idx + 1)
                    s.activation(hid_sb[g % 2][:, m * G:(m + 1) * G],
                                 gate_ps[m % 2][:, :], AF.Silu,
                                 bias=bhg_sb[:, m:m + 1]).then_inc(s_gate, 1)

        @blk.vector
        def _(v):
            v.memset(S_f32[:, :], 0.0)
            for g in range(NG):
                v.wait_ge(t_sim, g + 1)
                v.wait_ge(v_msk, 33)
                if g >= 2:
                    v.wait_ge(s_relu, g - 1)
                    v.wait_ge(v_atn, g - 1)
                v.tensor_add(sims_sb[g % 2][:, :], sim_ps[:, :],
                             biasT_sb[:, :]).then_inc(v_sim1, 1)
                v.wait_ge(s_relu, g + 1)
                v.tensor_mul(atn_sb[g][:, :], relu_sb[:, :],
                             sims_sb[g % 2][:, :]).then_inc(v_atn, 1)
            for i in range(NCORES):
                v.wait_ge(d_co[i % 2], 16 * (i // 2 + 1))
                v.scalar_tensor_tensor(S_f32[:, :], gath_sb[i % 2][:, :],
                                       pmask_sb[:, i:i + 1], S_f32[:, :],
                                       ALU.mult, ALU.add).then_inc(v_Ss, 1)
            v.tensor_copy(S_sb[:, :], S_f32[:, :]).then_inc(v_Ss, 1)
            for g in range(NG):
                if g >= 2:
                    v.wait_ge(t_wout, 4 * (g - 1))
                for m in range(ECH):
                    idx = ECH * g + m
                    v.wait_ge(t_B, idx + 1)
                    v.wait_ge(s_gate, idx + 1)
                    hsl = hid_sb[g % 2][:, m * G:(m + 1) * G]
                    v.tensor_mul(hsl, hsl, b_ps[m % 2][:, :]).then_inc(v_hid, 1)
                v.wait_ge(v_bc, 1)
                v.wait_ge(d_x0g[g % 2], 16 * (g // 2 + 1))
                if g >= 2:
                    v.wait_ge(d_x1w[g % 2], 16 * (g // 2))
                for p_ in range(4):
                    v.wait_ge(t_wout, 4 * g + p_ + 1)
                    for half in range(2):
                        o = 2 * p_ + half
                        dst = x2_sb[g % 2][:, o * G:(o + 1) * G]
                        v.tensor_mul(dst, dst,
                                     bcA1_sb[:, g * G:(g + 1) * G])
                        ins = v.tensor_add(
                            dst, dst, wo_ps[p_ % 2][:, half * 256:(half + 1) * 256])
                    ins.then_inc(v_x1, 1)

    if stage == 2:
        with nc.Block() as blk:
            @blk.sync
            def _(sy):
                sy.wait_ge(d_x1w[0], 16 * NG // 2)
                sy.wait_ge(d_x1w[1], 16 * NG // 2)
                if debug:
                    sy.dma_start(out=dbg1[:, :], in_=bcA1_sb[:, :]).then_inc(d_dbg, 16)
                    sy.dma_start(out=dbg5[:, :], in_=x2_sb[1][:, :]).then_inc(d_dbg, 16)
                    sy.dma_start(out=dbg4[:, 0:ECH * G], in_=hid_sb[1][:, :]).then_inc(d_dbg, 16)
                    sy.dma_start(out=dbg4[:, ECH * G:ECH * G + 512],
                                 in_=atn_sb[1][:, :]).then_inc(d_dbg, 16)
                    sy.dma_start(out=dbg2[:, :], in_=sims_sb[1][:, :]).then_inc(d_dbg, 16)
                    sy.dma_start(out=dbg4[:, 4608:6656],
                                 in_=wout_sb[:, 0:2048]).then_inc(d_dbg, 16)
                    sy.wait_ge(d_dbg, 96)
        es2.close(); ps2.close(); es012.close(); es.close()
        return nc

    es2.close(); es012.close()
    es3 = contextlib.ExitStack()
    n3_sb = mk(es3, "n3_sb", [128, KCH * HT], BF16)
    d_sb = mk(es3, "d_sb", [128, 32 * 512], BF16)
    bc3a_sb = mk(es3, "bc3a_sb", [128, T])
    bc3b_sb = mk(es3, "bc3b_sb", [128, T])
    es3a = contextlib.ExitStack()
    x1r_sb = [mk(es3a, f"x1r_sb{i}", [128, T], BF16) for i in range(2)]
    xsq3_sb = mk(es3a, "xsq3_sb", [128, T], BF16)
    q0 = mk(es3a, "q0", [128, T]); q1 = mk(es3a, "q1", [128, T])
    q2 = mk(es3a, "q2", [128, T])
    mk_sb = mk(es3a, "mk_sb", [128, 32 * 128], BF16)
    up_sb = mk(es3a, "up_sb", [128, 32 * 512], BF16)

    ps2.close()
    ps3s = contextlib.ExitStack()
    st31_ps = ps3s.enter_context(nc.psum_tensor("st31_ps", [128, T], F32))
    st32_ps = ps3s.enter_context(nc.psum_tensor("st32_ps", [128, T], F32))

    P2GATES = [(d_x1w[0], 16 * NG // 2), (d_x1w[1], 16 * NG // 2),
               (v_x1, 4 * NG), (s_gate, ECH * NG),
               (t_Bg, NG), (v_hid, ECH * NG), (v_atn, NG), (t_wout, 4 * NG)]

    with nc.Block() as blk:
        @blk.gpsimd
        def _(g2):
            for s_, v_ in P2GATES:
                g2.wait_ge(s_, v_)
            g2.dma_start(out=mk_sb[:, :],
                         in_=bass.AP(makerbd_d.tensor, 0,
                                     [[128, 128], [128 * 128, 32], [1, 128]])
                         ).then_inc(d_mk, 16)
            g2.dma_start(out=up_sb[:, :],
                         in_=bass.AP(upbd_d.tensor, 0,
                                     [[512, 128], [128 * 512, 32], [1, 512]])
                         ).then_inc(d_up[0], 16)

        @blk.sync
        def _(sy):
            for s_, v_ in P2GATES:
                sy.wait_ge(s_, v_)
            sy.wait_ge(d_up[1], 16 * NG)
            for ch in range(OCH):
                if ch >= 2:
                    sy.wait_ge(s_sq3, ch - 1)
                    sy.wait_ge(t_st3, ch - 1)
                sy.dma_start(
                    out=x1r_sb[ch % 2][:, :],
                    in_=bass.AP(x1b_sp.tensor, ch * G,
                                [[OCH * G, 128], [128 * OCH * G, NG], [1, G]]),
                ).then_inc(d_x1b[ch % 2], 16)


        @blk.scalar
        def _(s):
            for ch in range(OCH):
                s.wait_ge(d_x1b[ch % 2], 16 * (ch // 2 + 1))
                if ch >= 1:
                    s.wait_ge(t_st3, ch)
                s.activation(xsq3_sb[:, :], x1r_sb[ch % 2][:, :], AF.Square).then_inc(s_sq3, 1)
            s.wait_ge(v_r31, 1)
            s.activation(q1[:, :], q2[:, :], AF.Sqrt).then_inc(s_r31, 1)

        @blk.tensor
        def _(t):
            for s_, v_ in P2GATES:
                t.wait_ge(s_, v_)
            for ch in range(OCH):
                t.wait_ge(d_x1b[ch % 2], 16 * (ch // 2 + 1))
                for sl in range(4):
                    t.matmul(st31_ps[:, sl * 512:(sl + 1) * 512], onesmatb[:],
                             x1r_sb[ch % 2][:, sl * 512:(sl + 1) * 512],
                             start=(ch == 0), stop=(ch == OCH - 1))
                t.wait_ge(s_sq3, ch + 1)
                for sl in range(4):
                    ins = t.matmul(st32_ps[:, sl * 512:(sl + 1) * 512], onesmatb[:],
                                   xsq3_sb[:, sl * 512:(sl + 1) * 512],
                                   start=(ch == 0), stop=(ch == OCH - 1))
                ins.then_inc(t_st3, 1)

        @blk.vector
        def _(v):
            v.wait_ge(t_st3, OCH)
            v.tensor_scalar(q0[:, :], st31_ps[:, :], 1.0 / D, None, ALU.mult)
            v.tensor_scalar(q1[:, :], st32_ps[:, :], 1.0 / D, None, ALU.mult)
            v.tensor_mul(q2[:, :], q0[:, :], q0[:, :])
            v.tensor_sub(q1[:, :], q1[:, :], q2[:, :])
            v.tensor_scalar(q1[:, :], q1[:, :], EPS, None, ALU.add)
            v.reciprocal(q2[:, :], q1[:, :]).then_inc(v_r31, 1)
            v.wait_ge(s_r31, 1)
            v.tensor_mul(q0[:, :], q0[:, :], q1[:, :]).then_inc(v_rows3, 1)
            v.tensor_copy(bc3a_sb[:, :], q1[:, :]).then_inc(v_bc3, 1)
            v.tensor_copy(bc3b_sb[:, :], q0[:, :]).then_inc(v_bc3, 1)

    ps3s.close()

    ps3 = contextlib.ExitStack()
    z_ps = [ps3.enter_context(nc.psum_tensor(f"z_ps{i}", [128, 512], F32)) for i in range(4)]
    cp_ps = [ps3.enter_context(nc.psum_tensor(f"cp_ps{i}", [128, 512], F32))
             for i in range(4)]

    with nc.Block() as blk:
        @blk.tensor
        def _(t):
            t.wait_ge(d_mk, 16)
            t.wait_ge(d_up[0], 16)
            for i in range(32):
                if i >= 2:
                    t.wait_ge(v_D, i - 1)
                t.matmul(z_ps[i % 2][:, :], mk_sb[:, i * 128:(i + 1) * 128],
                         up_sb[:, i * 512:(i + 1) * 512],
                         start=True, stop=True).then_inc(t_D, 1)

        @blk.vector
        def _(v):
            for i in range(32):
                v.wait_ge(t_D, i + 1)
                v.tensor_copy(d_sb[:, i * 512:(i + 1) * 512], z_ps[i % 2][:, :]).then_inc(v_D, 1)

    es3a.close()
    es3b = contextlib.ExitStack()
    h_sb = mk(es3b, "h_sb", [128, KCH * HT], BF16)
    w3_sb = [mk(es3b, f"w3_sb{i}", [128, KCH * D], BF16) for i in range(2)]
    oacc_sb = mk(es3b, "oacc_sb", [128, OCH * HT])
    x1h_sb = [mk(es3b, f"x1h_sb{i}", [128, HT]) for i in range(4)]
    nt3_sb = mk(es3b, "nt3_sb", [128, HT])

    ES3A_GATES = [(t_st3, OCH), (s_sq3, OCH), (v_bc3, 2), (t_D, 32), (v_D, 32),
                  (v_rows3, 1)]

    with nc.Block() as blk:
        @blk.sync
        def _(sy):
            for s_, v_ in ES3A_GATES:
                sy.wait_ge(s_, v_)
            slot_evt = [None] * 4
            slot_i = 0
            def x1h_load(hf, ch, evt):
                nonlocal slot_i
                sl_ = slot_i % 4
                if slot_evt[sl_] is not None:
                    sy.wait_ge(*slot_evt[sl_])
                sy.dma_start(
                    out=x1h_sb[sl_][:, :],
                    in_=bass.AP(x1_sp.tensor, hf * 4 * 128 * OCH * G + ch * G,
                                [[OCH * G, 128], [128 * OCH * G, 4], [1, G]]),
                ).then_inc(d_x1h4[sl_], 16)
                slot_evt[sl_] = evt
                slot_i += 1
            for hf in range(NHF):
                for ch in range(OCH):
                    x1h_load(hf, ch, (v_n3, OCH * hf + ch + 1))
                for ch in range(OCH):
                    x1h_load(hf, ch, (v_out, OCH * hf + ch + 1))
                sy.wait_ge(v_out, OCH * (hf + 1))
                sy.dma_start(
                    out=bass.AP(out_d.tensor, hf * HT,
                                [[T, 128], [128 * T, OCH], [1, HT]]),
                    in_=oacc_sb[:, :],
                ).then_inc(d_out, 16)
            sy.wait_ge(d_out, 32)

        @blk.gpsimd
        def _(g_):
            for s_, v_ in ES3A_GATES:
                g_.wait_ge(s_, v_)
            for ks in range(NKS):
                if ks >= 2:
                    g_.wait_ge(t_cp, CPS * (ks - 1))
                sl = ks % NSL
                g_.dma_start(
                    out=w3_sb[ks % 2][:, :],
                    in_=chunkT(wct_d, FF // 128, D, ch0=sl * KCH, nsub=KCH),
                ).then_inc(d_w3[ks % 2], 16)

        @blk.tensor
        def _(t):
            zi = 0
            for ks in range(NKS):
                hf, sl = divmod(ks, NSL)
                t.wait_ge(v_n3, OCH * hf + min(2 * sl + 2, OCH))
                t.wait_ge(d_w3[ks % 2], 16 * (ks // 2 + 1))

                def zburst(nt):
                    nonlocal zi
                    for st in range(STR):
                        gst = sl * STR + st
                        for m in range(4):
                            if zi >= 4:
                                t.wait_ge(s_z, zi - 3)
                            t.matmul(z_ps[zi % 4][:, :],
                                     d_sb[:, gst * 512 + m * 128: gst * 512 + (m + 1) * 128],
                                     n3_sb[:, (gst % KCH) * HT + nt * 512:
                                           (gst % KCH) * HT + (nt + 1) * 512],
                                     start=True, stop=True).then_inc(t_z, 1)
                            zi += 1

                def cpgroup(qi, oh, nt):
                    for ot in range(4):
                        cpi = CPS * ks + qi * 4 + ot
                        if cpi >= 4:
                            t.wait_ge(v_acc, cpi - 3)
                        for k in range(KCH):
                            t.wait_ge(s_z, ZPS * ks + nt * 8 + k + 1)
                            ins = t.matmul(
                                cp_ps[ot][:, :],
                                w3_sb[ks % 2][:, k * D + oh * 512 + ot * 128:
                                              k * D + oh * 512 + (ot + 1) * 128],
                                h_sb[:, k * HT + nt * 512: k * HT + (nt + 1) * 512],
                                start=(k == 0), stop=(k == KCH - 1))
                        ins.then_inc(t_cp, 1)

                zburst(0)
                cpgroup(0, 0, 0)
                zburst(1)
                cpgroup(1, 1, 0)
                cpgroup(2, 0, 1)
                cpgroup(3, 1, 1)

        @blk.scalar
        def _(s):
            zi = 0
            for ks in range(NKS):
                if ks >= 1:
                    s.wait_ge(t_cp, CPS * ks)
                for nt in range(2):
                    for st in range(STR):
                        for m in range(4):
                            fk = st * 4 + m
                            s.wait_ge(t_z, zi + 1)
                            s.activation(h_sb[:, fk * HT + nt * 512:
                                              fk * HT + (nt + 1) * 512],
                                         z_ps[zi % 4][:, :], AF.Gelu).then_inc(s_z, 1)
                            zi += 1

        @blk.vector
        def _(v):
            for hf in range(NHF):
                for ch in range(OCH):
                    li = 2 * OCH * hf + ch
                    v.wait_ge(d_x1h4[li % 4], 16 * (li // 4 + 1))
                    v.tensor_mul(nt3_sb[:, :], x1h_sb[li % 4][:, :],
                                 bc3a_sb[:, hf * HT:(hf + 1) * HT])
                    v.tensor_sub(n3_sb[:, ch * HT:(ch + 1) * HT], nt3_sb[:, :],
                                 bc3b_sb[:, hf * HT:(hf + 1) * HT]).then_inc(v_n3, 1)
                if hf >= 1:
                    v.wait_ge(d_out, 16 * hf)
                for sl in range(NSL):
                    ks = hf * NSL + sl
                    for qi, (oh, nt) in enumerate([(0, 0), (1, 0), (0, 1), (1, 1)]):
                        for ot in range(4):
                            cpi = CPS * ks + qi * 4 + ot
                            v.wait_ge(t_cp, cpi + 1)
                            dst = oacc_sb[:, (oh * 4 + ot) * HT + nt * 512:
                                          (oh * 4 + ot) * HT + (nt + 1) * 512]
                            if sl == 0:
                                v.tensor_copy(dst, cp_ps[ot][:, :]).then_inc(v_acc, 1)
                            else:
                                v.tensor_add(dst, dst,
                                             cp_ps[ot][:, :]).then_inc(v_acc, 1)
                for ch in range(OCH):
                    li = 2 * OCH * hf + OCH + ch
                    v.wait_ge(d_x1h4[li % 4], 16 * (li // 4 + 1))
                    v.tensor_add(oacc_sb[:, ch * HT:(ch + 1) * HT],
                                 oacc_sb[:, ch * HT:(ch + 1) * HT],
                                 x1h_sb[li % 4][:, :]).then_inc(v_out, 1)

    es3b.close(); es3.close(); ps3.close(); es.close()
    return nc


def kernel(**inputs):
    in_maps = prep_inputs(inputs)
    nc = build_nc()
    res = run_bass_kernel_spmd(nc, in_maps, core_ids=list(range(NCORES)))
    shards = [np.ascontiguousarray(res.results[c]["out"].T) for c in range(NCORES)]
    return np.concatenate(shards, axis=0).reshape(1, 16384, 1024).astype(np.float32)
